# version 37
# baseline (speedup 1.0000x reference)
import numpy as np
from contextlib import ExitStack

import concourse.bass as bass
import concourse.dve_ops as dve_ops
import concourse.mybir as mybir
from concourse.bass_utils import run_bass_kernel_spmd

F32 = mybir.dt.float32
BF16 = mybir.dt.bfloat16
AF = mybir.ActivationFunctionType
ALU = mybir.AluOpType
AX = mybir.AxisListType

D = 1024
NC = 8
FD = 64
LN_EPS = 1e-5
ATTN_EPS = 1e-6

FULL_CFG = dict(T_OWN=2048, HALO=128, PPREF=6144, PTILE=768, TILE=512,
                N_CORES=8, N_SLICE=4)


class Sched:
    STREAMS = ("sync", "gdma", "sdma", "gcomp", "tensor", "vector", "scalar")
    ENGINE_OF = {"sync": "sync", "gdma": "gpsimd", "sdma": "scalar",
                 "gcomp": "gpsimd",
                 "tensor": "tensor", "vector": "vector", "scalar": "scalar"}

    def __init__(self, nc, es):
        self.nc = nc
        self.es = es
        self.seq = 0
        self.late_toks = set()
        self.ops = []
        self.sems = {}
        self.sem_stream = {}
        self.cnt = {}
        for s in ("gcomp", "tensor", "vector", "scalar"):
            self._new_sem(s, s)

    def _new_sem(self, name, stream):
        self.sems[name] = self.es.enter_context(self.nc.semaphore("sem_" + name))
        self.sem_stream[name] = stream
        self.cnt[name] = 0

    def emit(self, stream, fn, deps=(), inc=True, n=1, sem=None, late=False):
        dma = stream in ("sync", "gdma", "sdma")
        if sem is None:
            assert not dma, "DMA ops must name a slot semaphore"
            sem = stream
        if sem not in self.sems:
            self._new_sem(sem, stream)
        assert self.sem_stream[sem] == stream
        amt = 16 if dma else 1
        deps = [d for d in deps if d is not None]
        for d in deps:
            assert isinstance(d, tuple) and len(d) == 2 and isinstance(d[0], str), \
                f"bad dep {d!r} on stream {stream} (op #{self.seq})"
        tok = None
        if inc:
            self.cnt[sem] += amt * (n if dma else 1)
            tok = (sem, self.cnt[sem])
        self.ops.append((self.seq, stream, sem, fn, deps, tok, n if dma else 1, amt))
        if late and tok is not None:
            self.late_toks.add(tok)
        self.seq += 1
        return tok

    def barrier(self):
        return [(s, c) for s, c in self.cnt.items() if c > 0]

    def finalize(self, block):
        engines = {}
        for op in self.ops:
            engines.setdefault(self.ENGINE_OF[op[1]], []).append(op)
        for eng_attr, ops in engines.items():
            my_streams = [s for s in self.STREAMS if self.ENGINE_OF[s] == eng_attr]

            def section(eng, ops=ops, my_streams=my_streams):
                waited = {}
                for _, stream, sem, fn, deps, tok, n, amt in ops:
                    for (ds, dv) in deps:
                        if waited.get(ds, 0) >= dv:
                            continue
                        eng.wait_ge(self.sems[ds], dv)
                        waited[ds] = dv
                    ret = fn(eng)
                    if tok is not None:
                        insts = ret if isinstance(ret, list) else [ret]
                        if stream in ("sync", "gdma"):
                            assert len(insts) == n, \
                                f"{stream} op declared n={n} but emitted {len(insts)}"
                            for inst in insts:
                                inst.then_inc(self.sems[sem], amt)
                        else:
                            assert insts[-1] is not None
                            insts[-1].then_inc(self.sems[sem], amt)
                return None

            getattr(block, eng_attr)(section)


def _bc_rep(t, C, n=NC, w=512):
    return bass.AP(t, 0, [[w, 128], [0, n], [1, C]])


def _bc_col(t, C, n=NC):
    return bass.AP(t, 0, [[n, 128], [1, n], [0, C]])


def _nsplit(c, step=512):
    out, p = [], 0
    while p < c:
        w = min(step, c - p)
        out.append((p, p + w))
        p += w
    return out


def build_program(cfg):
    T_OWN = cfg["T_OWN"]; HALO = cfg["HALO"]; PPREF = cfg["PPREF"]
    PTILE = cfg["PTILE"]; TILE = cfg["TILE"]
    T_LOC = T_OWN + HALO
    n_atile = T_OWN // TILE
    n_ptile = PPREF // PTILE
    btiles = []
    pos = HALO - 1
    first = True
    while pos < T_LOC:
        w = min(513 if first else 512, T_LOC - pos)
        btiles.append((pos, w))
        pos += w
        first = False
    BMAX = max(w for _, w in btiles)

    nc = bass.Bass("TRN2", target_bir_lowering=False)

    dti = {}
    def din(name, shape, dtype=F32):
        dti[name] = nc.dram_tensor(name, shape, dtype, kind="ExternalInput").ap()
        return dti[name]

    xm_f = din("xm_f", [128, NC, T_LOC])
    xp_tok = din("xp_tok", [PPREF // 128, 128, D])
    xp_f = din("xp_f", [128, NC, PPREF])
    pmask = din("pmask", [1, PPREF])
    hmaskr = din("hmaskr", [1, HALO])
    hscale = din("hscale", [128, 1])
    identf = din("identf", [128, 128])
    ln1w8 = din("ln1w8", [128, NC]); ln1b8 = din("ln1b8", [128, NC])
    ln2w8 = din("ln2w8", [128, NC]); ln2b8 = din("ln2b8", [128, NC])
    mur8 = din("mur8", [128, NC]); muk8 = din("muk8", [128, NC])
    ob8 = din("ob8", [128, NC]); f2b8 = din("f2b8", [128, NC])
    f1b32 = din("f1b32", [128, 32])
    qb_row = din("qb_row", [1, FD]); kb_row = din("kb_row", [1, FD])
    vb_row = din("vb_row", [1, D])
    qwT = din("qwT", [128, NC, FD])
    kwT = din("kwT", [128, NC, FD])
    vwT = din("vwT", [128, NC, D])
    owP = din("owP", [NC, 128, NC, 128])
    f1P = din("f1P", [32, 128, NC, 128])
    f2P = din("f2P", [NC, 128, 32, 128])
    wrP = din("wrP", [NC, 128, NC, 128])
    wkP = din("wkP", [NC, 128, NC, 128])
    wvP = din("wvP", [NC, 128, NC, 128])
    ow16 = nc.dram_tensor("ow16", [NC, 128, NC, 128], BF16, kind="Internal").ap()
    f1s = nc.dram_tensor("f1s", [16, 128, 2, NC, 128], BF16, kind="Internal").ap()
    f2s = nc.dram_tensor("f2s", [NC, 128, 32, 128], BF16, kind="Internal").ap()
    wr16 = nc.dram_tensor("wr16", [NC, 128, NC, 128], BF16, kind="Internal").ap()
    wk16 = nc.dram_tensor("wk16", [NC, 128, NC, 128], BF16, kind="Internal").ap()
    wv16 = nc.dram_tensor("wv16", [NC, 128, NC, 128], BF16, kind="Internal").ap()
    y2f = nc.dram_tensor("y2f", [128, NC, T_OWN], F32, kind="ExternalOutput").ap()
    dbg = nc.dram_tensor("dbg", [64, 2080], F32, kind="ExternalOutput").ap() \
        if cfg.get("DBG") else None

    esg = ExitStack()
    S = lambda name, shape, dtyp=F32: esg.enter_context(nc.sbuf_tensor(name, shape, dtyp))
    esPA = ExitStack()
    Spa = lambda name, shape, dtyp=F32: esPA.enter_context(nc.sbuf_tensor(name, shape, dtyp))

    sT = S("sT", [128, NC, T_LOC])
    kvF = S("kvF", [64, D]); kvB = S("kvB", [64, D], BF16)
    ksF = S("ksF", [64, 1]); ksB = S("ksB", [64, 1], BF16)
    c_identf = S("c_identf", [128, 128])
    c_identb = S("c_identb", [128, 128], BF16)
    c_maskT = S("c_maskT", [128, 128])
    c_ones_b = S("c_ones_b", [128, 1], BF16)
    c_ones_f = S("c_ones_f", [1, 128])
    c_zcol = S("c_zcol", [128, 1], BF16)
    c_zrow = S("c_zrow", [1, 64], BF16)
    c_ln1w = S("c_ln1w", [128, NC]); c_ln1b = S("c_ln1b", [128, NC])
    c_ln1b_b = S("c_ln1b_b", [128, NC], BF16)
    c_ln2w = S("c_ln2w", [128, NC]); c_ln2b = S("c_ln2b", [128, NC])
    c_mur = S("c_mur", [128, NC]); c_muk = S("c_muk", [128, NC])
    c_ob = S("c_ob", [128, NC]); c_f2b = S("c_f2b", [128, NC])
    c_f1b = S("c_f1b", [128, 32])
    c_qb = S("c_qb", [1, FD]); c_kb = S("c_kb", [1, FD])
    c_cfq = S("c_cfq", [64, 1]); c_cfk = S("c_cfk", [64, 1])
    c_cfv = S("c_cfv", [1, D])
    c_cfv_rep = S("c_cfv_rep", [128, D], BF16)
    c_negs = S("c_negs", [64, 1])
    c_hmrow = S("c_hmrow", [1, HALO])
    c_hscale = S("c_hscale", [128, 1])
    mean_row = S("mean_row", [1, 768])
    var_row = S("var_row", [1, 516])
    r2_row = S("r2_row", [1, 516])
    rstd_row = S("rstd_row", [1, 768])
    srow_sb = S("srow_sb", [1, 516])
    repPm = S("repPm", [128, 512]); repPr = S("repPr", [128, 512])
    repQm = S("repQm", [128, 512]); repQr = S("repQr", [128, 512])
    dcol = S("dcol", [128, 4])
    qwp = Spa("qwp", [128, NC, FD], BF16)
    kwp = Spa("kwp", [128, NC, FD], BF16)
    vwp = Spa("vwp", [128, NC, D], BF16)
    qt_sb = Spa("qt_sb", [64, TILE], BF16); kt_sb = Spa("kt_sb", [64, TILE], BF16)
    smt_sb = Spa("smt_sb", [128, 128], BF16)
    ktok_sb = Spa("ktok_sb", [128, 64], BF16)
    kA = Spa("kA", [64, 512]); kB = Spa("kB", [64, 512])
    rd_row = Spa("rd_row", [1, 128])
    drep_sb = Spa("drep_sb", [128, 128])
    c_hmrep = Spa("c_hmrep", [64, HALO])

    psA = esg.enter_context(nc.psum_tensor("psA", [128, 512], F32))
    psBC = esg.enter_context(nc.psum_tensor("psBC", [128, 1024], F32))
    psD = esg.enter_context(nc.psum_tensor("psD", [128, 512], F32))
    psE = esg.enter_context(nc.psum_tensor("psE", [128, 512], F32))
    psF = esg.enter_context(nc.psum_tensor("psF", [128, 512], F32))
    psG = esg.enter_context(nc.psum_tensor("psG", [128, 512], F32))
    psH = esg.enter_context(nc.psum_tensor("psH", [128, 1024], BF16))

    sch = Sched(nc, esg)
    E = sch.emit

    tl = []
    def ld(dst, src):
        tl.append(E("sync", lambda e, d=dst, s=src: e.dma_start(out=d, in_=s),
                    sem="ld_setup"))
    ld(c_identf[:, :], identf[:, :])
    ld(c_ln1w[:, :], ln1w8[:, :]); ld(c_ln1b[:, :], ln1b8[:, :])
    ld(c_ln2w[:, :], ln2w8[:, :]); ld(c_ln2b[:, :], ln2b8[:, :])
    ld(c_mur[:, :], mur8[:, :]); ld(c_muk[:, :], muk8[:, :])
    ld(c_ob[:, :], ob8[:, :]); ld(c_f2b[:, :], f2b8[:, :]); ld(c_f1b[:, :], f1b32[:, :])
    ld(c_qb[:, :], qb_row[:, :]); ld(c_kb[:, :], kb_row[:, :]); ld(c_cfv[:, :], vb_row[:, :])
    ld(c_hscale[:, :], hscale[:, :])
    t_hm = E("sync", lambda e: e.dma_start(out=c_hmrow[0:1, 0:HALO], in_=hmaskr[:, :]),
             sem="ld_setup")
    tld = t_hm
    E("gdma", lambda e: e.dma_start(out=qwp[:, :, :], in_=qwT[:, :, :]), sem="g_setup")
    E("gdma", lambda e: e.dma_start(out=kwp[:, :, :], in_=kwT[:, :, :]), sem="g_setup")
    E("gdma", lambda e: e.dma_start(out=vwp[:, :, :], in_=vwT[:, :, :]), sem="g_setup")
    t_idb = E("gdma", lambda e: e.dma_start(out=c_identb[:, :], in_=identf[:, :]),
              sem="g_setup")
    t_qw = t_kw = t_vw = t_idb
    t_ones = E("gcomp", lambda e: e.memset(c_ones_b[:, :], 1.0))
    t_onef = E("gcomp", lambda e: e.memset(c_ones_f[:, :], 1.0))
    t_zc = E("gcomp", lambda e: [e.memset(c_zcol[:, :], 0.0),
                                 e.memset(c_zrow[:, :], 0.0)][-1])

    t_io1 = E("gcomp", lambda e: e.iota(repPm[:, 0:128], [[1, 128]], channel_multiplier=0,
                                        allow_small_or_imprecise_dtypes=True))
    t_io2 = E("gcomp", lambda e: e.iota(repPm[:, 128:256], [[0, 128]], channel_multiplier=1,
                                        allow_small_or_imprecise_dtypes=True))
    t_msk = E("vector", lambda e: e.tensor_tensor(c_maskT[:, :], repPm[:, 128:256],
                                                  repPm[:, 0:128], op=ALU.is_le),
              deps=[t_io1, t_io2])

    t_l1b = E("vector", lambda e: e.tensor_copy(c_ln1b_b[:, :], c_ln1b[:, :]), deps=[tld])

    cf_chain = [None]
    def emit_cf(w_bf, dep_w, brow, width, col_out=None, row_out=None):
        outs = []
        prev = cf_chain[0]
        for (h0, h1) in _nsplit(width):
            cw = h1 - h0
            t_mm = E("tensor", lambda e, h0=h0, cw=cw, w_bf=w_bf: [
                e.matmul(psA[0:1, 0:cw], c_ln1b_b[:, a:a + 1], w_bf[:, a, h0:h0 + cw],
                         start=(a == 0), stop=(a == NC - 1)) for a in range(NC)][-1],
                     deps=[t_l1b, dep_w] + ([prev] if prev else []))
            t_add = E("vector", lambda e, h0=h0, cw=cw: e.tensor_tensor(
                (row_out if row_out is not None else srow_sb)[0:1, h0:h0 + cw],
                psA[0:1, 0:cw], brow[0:1, h0:h0 + cw], op=ALU.add), deps=[t_mm, tld])
            prev = t_add
            outs.append(t_add)
        if col_out is not None:
            t_tr = E("tensor", lambda e: e.matmul(
                psG[0:width, 0:1], (row_out if row_out is not None else srow_sb)[0:1, 0:width],
                c_ones_f[0:1, 0:1], start=True, stop=True), deps=outs + [t_onef])
            t_cp = E("vector", lambda e: e.tensor_copy(col_out[:, :], psG[0:width, 0:1]),
                     deps=[t_tr])
            cf_chain[0] = t_cp
            return t_cp
        cf_chain[0] = outs[-1]
        return outs[-1]

    t_cfq = emit_cf(qwp, t_qw, c_qb, FD, col_out=c_cfq)
    t_cfk = emit_cf(kwp, t_kw, c_kb, FD, col_out=c_cfk)
    t_cfvr = emit_cf(vwp, t_vw, c_cfv, D, row_out=c_cfv)

    fold_toks = []
    for a in range(NC):
        fold_toks.append(E("vector", lambda e, a=a: e.tensor_scalar_mul(
            qwp[:, a, :], qwp[:, a, :], c_ln1w[:, a:a + 1]), deps=[t_qw, tld, t_cfq]))
        fold_toks.append(E("vector", lambda e, a=a: e.tensor_scalar_mul(
            kwp[:, a, :], kwp[:, a, :], c_ln1w[:, a:a + 1]), deps=[t_kw, tld, t_cfk]))
        fold_toks.append(E("vector", lambda e, a=a: e.tensor_scalar_mul(
            vwp[:, a, :], vwp[:, a, :], c_ln1w[:, a:a + 1]), deps=[t_vw, tld, t_cfvr]))
    t_fold = fold_toks[-1]

    cfvb = []
    for h in range(0, D, 512):
        t_mm = E("tensor", lambda e, h=h: e.matmul(psBC[:, h:h + 512], c_ones_f[0:1, :],
                                                   c_cfv[0:1, h:h + 512], start=True, stop=True),
                 deps=[t_cfvr, t_onef])
        cfvb.append(E("vector", lambda e, h=h: e.tensor_copy(
            c_cfv_rep[:, h:h + 512], psBC[:, h:h + 512]), deps=[t_mm]))
    t_cfv = cfvb[-1]

    t_smm = E("tensor", lambda e: [e.matmul(psA[0:1, 0:FD], c_ones_b[:, 0:1], kwp[:, a, :],
                                            start=(a == 0), stop=(a == NC - 1))
                                   for a in range(NC)][-1],
              deps=[t_fold, t_ones, cf_chain[0]])
    t_sng = E("vector", lambda e: e.tensor_scalar_mul(srow_sb[0:1, 0:FD], psA[0:1, 0:FD],
                                                      -1.0), deps=[t_smm])
    t_str = E("tensor", lambda e: e.matmul(psG[0:FD, 0:1], srow_sb[0:1, 0:FD],
                                           c_ones_f[0:1, 0:1], start=True, stop=True),
              deps=[t_sng, t_onef, cf_chain[0]])
    t_negs = E("vector", lambda e: e.tensor_copy(c_negs[:, :], psG[0:FD, 0:1]), deps=[t_str])

    t_hmm = E("tensor", lambda e: e.matmul(psG[0:64, 128:128 + HALO], c_ones_f[0:1, 0:64],
                                           c_hmrow[0:1, 0:HALO], start=True, stop=True),
              deps=[t_hm, t_onef, t_negs])
    t_hmr = E("vector", lambda e: e.tensor_copy(c_hmrep[:, :], psG[0:64, 128:128 + HALO]),
              deps=[t_hmm])

    t_i2 = E("vector", lambda e: e.memset(t2acc[:, :], 0.0))
    t_i3 = E("vector", lambda e: e.memset(t3acc[:, :], 0.0))

    setup_done = sch.barrier()

    def emit_stats(xb3, sq3, C, deps_x, deps_sq_slot, deps_psA, sq_tok=None,
                   deps_rows=(), row_off=0, tmp_off=None, rows_eng="vector"):
        o = row_off
        ot = row_off if tmp_off is None else tmp_off
        if sq_tok is None:
            t_sq = E("scalar", lambda e: e.activation(sq3, xb3, AF.Square),
                     deps=deps_x + deps_sq_slot)
        else:
            t_sq = sq_tok
        t_s = E("tensor", lambda e: [e.matmul(psA[0:1, 0:C], c_ones_b[:, 0:1],
                                              xb3[:, a, :], start=(a == 0), stop=(a == NC - 1))
                                     for a in range(NC)][-1],
                deps=deps_x + [t_ones] + deps_psA)
        t_q = E("tensor", lambda e: [e.matmul(psA[32:33, 0:C], c_ones_b[:, 0:1],
                                              sq3[:, a, :], start=(a == 0), stop=(a == NC - 1))
                                     for a in range(NC)][-1], deps=[t_sq])
        if rows_eng == "scalar":
            t_m = E("scalar", lambda e: e.activation(mean_row[0:1, o:o + C],
                                                     psA[0:1, 0:C], AF.Copy,
                                                     scale=1.0 / D),
                    deps=[t_s, t_q] + list(deps_rows))
        else:
            t_m = E("vector", lambda e: e.tensor_scalar_mul(mean_row[0:1, o:o + C],
                                                            psA[0:1, 0:C], 1.0 / D),
                    deps=[t_s, t_q] + list(deps_rows))
        t_m2 = E("scalar", lambda e: e.activation(srow_sb[0:1, ot:ot + C],
                                                  mean_row[0:1, o:o + C],
                                                  AF.Square), deps=[t_m])
        if rows_eng == "scalar":
            t_v = E("scalar", lambda e: e.activation(var_row[0:1, ot:ot + C],
                                                     psA[32:33, 0:C], AF.Copy,
                                                     scale=1.0 / D),
                    deps=[t_q, t_m] + list(deps_rows))
        else:
            t_v = E("vector", lambda e: e.tensor_scalar_mul(var_row[0:1, ot:ot + C],
                                                            psA[32:33, 0:C], 1.0 / D),
                    deps=[t_q] + list(deps_rows))
        t_u = E("vector", lambda e: e.scalar_tensor_tensor(var_row[0:1, ot:ot + C],
                                                           var_row[0:1, ot:ot + C],
                                                           LN_EPS, srow_sb[0:1, ot:ot + C],
                                                           op0=ALU.add, op1=ALU.subtract),
                deps=[t_v, t_m2])
        t_l = E("scalar", lambda e: e.activation(r2_row[0:1, ot:ot + C], var_row[0:1, ot:ot + C],
                                                 AF.Ln), deps=[t_u])
        t_rstd = E("scalar", lambda e: e.activation(rstd_row[0:1, o:o + C],
                                                    r2_row[0:1, ot:ot + C],
                                                    AF.Exp, scale=-0.5), deps=[t_l])
        return t_rstd, t_v

    def emit_bcast(ps_region, row_ap, deps, parts=128):
        return E("tensor", lambda e: e.matmul(ps_region, c_ones_f[0:1, 0:parts], row_ap,
                                              start=True, stop=True), deps=deps + [t_onef])

    def emit_reps(rep_m, rep_r, C, deps, deps_psA, deps_bufs, row_off=0):
        o = row_off
        t_b1 = E("tensor", lambda e: e.matmul(psA[:, 0:C], c_ones_f[0:1, :],
                                              mean_row[0:1, o:o + C], start=True, stop=True),
                 deps=deps + deps_psA + [t_onef])
        t_c1 = E("scalar", lambda e: e.copy(rep_m[:, 0:C], psA[:, 0:C]),
                 deps=[t_b1] + deps_bufs)
        t_b2 = E("tensor", lambda e: e.matmul(psA[:, 0:C], c_ones_f[0:1, :],
                                              rstd_row[0:1, o:o + C], start=True, stop=True),
                 deps=deps + [t_c1])
        t_c2 = E("scalar", lambda e: e.copy(rep_r[:, 0:C], psA[:, 0:C]),
                 deps=[t_b2] + deps_bufs)
        return t_c1, t_c2

    esp = ExitStack()
    Sp = lambda name, shape, dtyp=F32: esp.enter_context(nc.sbuf_tensor(name, shape, dtyp))
    NSUBP = PTILE // 128
    xpfb0 = Sp("xpfb0", [128, NC, PTILE], BF16)
    xpfb1 = Sp("xpfb1", [128, NC, PTILE], BF16)
    psq = Sp("psq", [128, NC, PTILE], BF16)
    xtokb0 = Sp("xtokb0", [128, NSUBP, 1024], BF16)
    k1tokB = Sp("k1tokB", [128, NSUBP, 64], BF16)
    kAp = Sp("kAp", [64, PTILE]); kBp = Sp("kBp", [64, PTILE])
    kmB = Sp("kmB", [64, PTILE], BF16); k1B = Sp("k1B", [64, PTILE], BF16)
    c_t2row = Sp("c_t2row", [1, 64])
    t2acc = Sp("t2acc", [64, 1]); t3acc = Sp("t3acc", [64, 1])
    a1p = Sp("a1p", [64, D], BF16)
    a1pT = Sp("a1pT", [128, NC, 64], BF16)
    pm_sb = Sp("pm_sb", [1, PTILE])
    prepM0 = Sp("prepM0", [128, PTILE], BF16); prepR0 = Sp("prepR0", [128, PTILE])
    prepQ0 = Sp("prepQ0", [128, PTILE], BF16)

    pstate = {"xpf0": None, "xpf1": None, "repP0": None, "repP1": None,
              "repQ0": None, "repQ1": None, "xtok0": None, "xtok1": None,
              "xq_sq": None, "xq_tok": None, "pm": None, "rows": None,
              "psA": None, "repP": None, "repQ": None, "kscr": None,
              "psD": None, "psH": None, "t2": t_i2, "t3": t_i3}
    a1_open = [False]
    t_a1_last = None

    pload = {}

    def issue_ploads(j, extra=()):
        if j >= n_ptile:
            return
        xkey = ("xpf0", "xpf1")[j % 2]
        xpfb = (xpfb0, xpfb1)[j % 2]
        t_xf = E("gdma", lambda e, j=j, xpfb=xpfb: [e.dma_start(
            out=xpfb[:, a, :], in_=xp_f[:, a, j * PTILE:(j + 1) * PTILE])
            for a in range(NC)],
            deps=([pstate[xkey]] if pstate[xkey] else []) + list(extra), n=NC, sem=xkey)
        pload[j] = t_xf

    krw = {}

    def emit_kraw(j, extra=()):
        if j >= n_ptile:
            return
        xpfb = (xpfb0, xpfb1)[j % 2]
        xkey = ("xpf0", "xpf1")[j % 2]
        t = E("tensor", lambda e, xpfb=xpfb: [e.matmul(
            (psE if u0 == 0 else psF)[0:64, 0:u1 - u0], kwp[:, a, :],
            xpfb[:, a, u0:u1], start=(a == 0), stop=(a == NC - 1))
            for (u0, u1) in _nsplit(PTILE) for a in range(NC)][-1],
            deps=[pload[j], t_fold]
            + ([pstate["psD"]] if pstate["psD"] else []) + list(extra))
        pstate[xkey] = t
        krw[j] = t

    sqtoks = {}

    def emit_p_sq(j):
        if j >= n_ptile:
            return
        xpfb = (xpfb0, xpfb1)[j % 2]
        sql = []
        for (u0, u1) in _nsplit(PTILE):
            t_sq = E("scalar", lambda e, u0=u0, u1=u1, xpfb=xpfb: e.activation(
                psq[:, :, u0:u1], xpfb[:, :, u0:u1], AF.Square),
                deps=[pload[j]] + ([pstate["xq_sq"]] if pstate["xq_sq"] else []))
            sql.append(t_sq)
        sqtoks[j] = sql

    def emit_p_stats(j, d0=()):
        if j >= n_ptile:
            return None
        d0 = list(d0)
        t_pm = E("sync", lambda e, j=j: e.dma_start(out=pm_sb[0:1, 0:PTILE],
                                                    in_=pmask[:, j * PTILE:(j + 1) * PTILE]),
                 deps=([pstate["pm"]] if pstate["pm"] else []) + d0, sem="pm")
        xpfb = (xpfb0, xpfb1)[j % 2]
        t_xf = pload[j]
        spans = []
        prev_r = pstate["rows"]
        sql = sqtoks.pop(j)
        for si, (u0, u1) in enumerate(_nsplit(PTILE)):
            cw = u1 - u0
            t_rows, t_psa = emit_stats(xpfb[:, :, u0:u1], psq[:, :, u0:u1], cw,
                                       [t_xf] + d0, [],
                                       ([pstate["psA"]] if pstate["psA"] else []),
                                       sq_tok=sql[si],
                                       deps_rows=([prev_r] if prev_r else []),
                                       row_off=u0, tmp_off=0, rows_eng="scalar")
            pstate["xq_sq"] = t_rows
            pstate["psA"] = t_psa
            prev_r = t_rows
            spans.append((u0, u1, t_rows))
        return dict(j=j, t_pm=t_pm, t_xf=t_xf, xpfb=xpfb, spans=spans)

    def emit_p_reps(stp):
        t_pm = stp["t_pm"]
        rep_toks = []
        for (u0, u1, t_rows) in stp["spans"]:
            cw = u1 - u0
            t_mrr = E("vector", lambda e, u0=u0, u1=u1: e.tensor_tensor(
                mean_row[0:1, u0:u1], mean_row[0:1, u0:u1], rstd_row[0:1, u0:u1],
                op=ALU.mult), deps=[t_rows])
            t_b1 = E("tensor", lambda e, u0=u0, u1=u1, cw=cw: e.matmul(
                psA[:, 0:cw], c_ones_f[0:1, :], mean_row[0:1, u0:u1], start=True,
                stop=True),
                     deps=[t_mrr, t_onef] + ([pstate["psA"]] if pstate["psA"] else []))
            t_c1 = E("scalar", lambda e, u0=u0, u1=u1: e.copy(
                prepM0[:, u0:u1], psA[:, 0:u1 - u0]),
                     deps=[t_b1] + ([pstate["repP0"]] if pstate["repP0"] else []))
            t_b2 = E("tensor", lambda e, u0=u0, u1=u1, cw=cw: e.matmul(
                psA[:, 0:cw], c_ones_f[0:1, :], rstd_row[0:1, u0:u1], start=True,
                stop=True), deps=[t_rows, t_c1])
            t_c2 = E("scalar", lambda e, u0=u0, u1=u1: e.copy(
                prepR0[:, u0:u1], psA[:, 0:u1 - u0]),
                     deps=[t_b2] + ([pstate["repP0"]] if pstate["repP0"] else []))
            t_b3 = E("tensor", lambda e, u0=u0, cw=cw: e.matmul(
                psA[:, 0:cw], c_ones_f[0:1, :], pm_sb[0:1, u0:u0 + cw], start=True,
                stop=True), deps=[t_pm, t_c2])
            pstate["pm"] = t_b3
            t_c3 = E("scalar", lambda e, u0=u0, u1=u1: e.copy(
                prepQ0[:, u0:u1], psA[:, 0:u1 - u0]),
                     deps=[t_b3] + ([pstate["repQ0"]] if pstate["repQ0"] else []))
            pstate["psA"] = t_c3
            rep_toks.append((t_c1, t_c2, t_c3))
        pstate["rows"] = rep_toks[-1][1]
        stp["rep_toks"] = rep_toks

    def emit_p_s2(stp):
        j = stp["j"]
        xtb = xtokb0
        prepM = prepM0
        prepR = prepR0
        prepQ = prepQ0
        rkey = "repP0"
        qkey = "repQ0"
        t_c1 = stp["rep_toks"][-1][0]; t_c2 = stp["rep_toks"][-1][1]
        t_c3 = stp["rep_toks"][-1][2]
        all_reps = [t for trio in stp["rep_toks"] for t in trio]
        t_xt = E("gdma", lambda e, j=j, xtb=xtb: e.dma_start(
            out=xtb[:, :, :],
            in_=xp_tok[j * NSUBP:(j + 1) * NSUBP].rearrange("n p d -> p n d")),
            deps=([pstate["xq_tok"]] if pstate["xq_tok"] else []), sem="xtok")
        conv_step(1)
        t_kraw = krw[j]
        emit_p_sq(j + 1)
        t_a0 = E("vector", lambda e: e.tensor_tensor(kAp[:, 0:512], psE[0:64, 0:512],
                                                     prepR[0:64, 0:512], op=ALU.mult),
                 deps=[t_kraw] + all_reps + ([pstate["kscr"]] if pstate["kscr"] else []))
        t_a = E("vector", lambda e: e.tensor_tensor(kAp[:, 512:PTILE], psF[0:64, 0:PTILE - 512],
                                                    prepR[0:64, 512:PTILE], op=ALU.mult),
                deps=[t_a0]) if PTILE > 512 else t_a0
        t_z = E("vector", lambda e: e.scalar_tensor_tensor(kBp[:, 0:PTILE],
                                                           prepM[0:64, 0:PTILE],
                                                           c_negs[:, :], kAp[:, 0:PTILE],
                                                           op0=ALU.mult, op1=ALU.add),
                deps=[t_a, t_negs])
        t_rel = E("vector", lambda e: e.tensor_scalar(kAp[:, 0:PTILE], kBp[:, 0:PTILE],
                                                      c_cfk[:, :], 0.0, op0=ALU.add,
                                                      op1=ALU.max), deps=[t_z, t_cfk])
        t_mn = E("vector", lambda e: e.tensor_scalar(kmB[:, 0:PTILE], kBp[:, 0:PTILE],
                                                     c_cfk[:, :], 0.0, op0=ALU.add,
                                                     op1=ALU.min), deps=[t_z, t_cfk])
        t_ex = E("scalar", lambda e: e.activation(kmB[:, 0:PTILE], kmB[:, 0:PTILE], AF.Exp),
                 deps=[t_mn])
        t_K = E("vector", lambda e: e.tensor_tensor(kAp[:, 0:PTILE], kAp[:, 0:PTILE],
                                                    kmB[:, 0:PTILE], op=ALU.add),
                deps=[t_rel, t_ex])
        t_km = E("vector", lambda e: e.tensor_tensor(kmB[:, 0:PTILE], kAp[:, 0:PTILE],
                                                     prepQ[0:64, 0:PTILE], op=ALU.mult),
                 deps=[t_K, t_c3])
        t_k1 = E("vector", lambda e: e.tensor_tensor(k1B[:, 0:PTILE], kmB[:, 0:PTILE],
                                                     prepR[0:64, 0:PTILE], op=ALU.mult),
                 deps=[t_km])
        t_t2r = E("vector", lambda e: e.reduce_sum(kBp[:, 0:1], kmB[:, 0:PTILE], axis=AX.X),
                  deps=[t_km], late=True)
        t_t2 = E("vector", lambda e: e.tensor_tensor(t2acc[:, :], t2acc[:, :], kBp[:, 0:1],
                                                     op=ALU.add), deps=[t_t2r, pstate["t2"]])
        pstate["t2"] = t_t2
        t_t3m = E("vector", lambda e: e.tensor_tensor(kAp[:, 0:PTILE], kmB[:, 0:PTILE],
                                                      prepM[0:64, 0:PTILE], op=ALU.mult),
                  deps=[t_km, t_c1, t_t2])
        t_t3r = E("vector", lambda e: e.reduce_sum(kBp[:, 1:2], kAp[:, 0:PTILE], axis=AX.X),
                  deps=[t_t3m], late=True)
        t_t3 = E("vector", lambda e: e.tensor_tensor(t3acc[:, :], t3acc[:, :], kBp[:, 1:2],
                                                     op=ALU.add), deps=[t_t3r, pstate["t3"]])
        pstate["t3"] = t_t3
        pstate[rkey] = t_t3m
        pstate[qkey] = t_km
        pstate["kscr"] = t_t3r
        pstate["psD"] = t_a

        sp_next = emit_p_stats(j + 1)
        emit_kraw(j + 1)
        issue_ploads(j + 2)

        tr_toks = []
        for s in range(NSUBP):
            tr_toks.append(E("tensor", lambda e, s=s: e.transpose(
                psH[:, s * 64:(s + 1) * 64], k1B[:, s * 128:(s + 1) * 128],
                c_identb[0:64, 0:64]),
                deps=[t_k1, t_idb] + ([pstate["psH"]] if pstate["psH"] and s == 0 else [])))
        t_kev = E("vector", lambda e: e.tensor_copy(
            k1tokB[:, :, :], psH[:, 0:NSUBP * 64].rearrange("p (n f) -> p n f", n=NSUBP)),
            deps=tr_toks)
        pstate["psH"] = t_kev
        first = not a1_open[0]
        t_mm = E("tensor", lambda e, first=first, xtb=xtb: [
            e.matmul(psBC[0:64, h:h + 512], k1tokB[:, s, :], xtb[:, s, h:h + 512],
                     start=(first and s == 0), stop=False)
            for s in range(NSUBP) for h in (0, 512)][-1],
            deps=[t_kev, t_xt])
        a1_open[0] = True
        pstate["xq_tok"] = t_mm
        if sp_next is not None:
            emit_p_reps(sp_next)
        return t_mm, sp_next

    cnv = {}
    conv_q = []

    def _conv_piece(fam, i):
        if fam == "f1":
            cnv["f1"] = E("gdma", lambda e, i=i: e.dma_start(
                out=f1s[i], in_=f1P[2 * i:2 * i + 2].rearrange("t p a h -> p t a h")),
                sem="cnv_f1")
        else:
            dst, srcw = {"ow": (ow16, owP), "f2": (f2s, f2P), "wr": (wr16, wrP),
                         "wk": (wk16, wkP), "wv": (wv16, wvP)}[fam]
            cnv[fam] = E("gdma", lambda e, dst=dst, srcw=srcw, i=i: e.dma_start(
                out=dst[i], in_=srcw[i]), sem="cnv_" + fam)

    def _conv(fam):
        n = 16 if fam == "f1" else NC
        conv_q.extend((fam, i) for i in range(n))

    def conv_step(k=1):
        for _ in range(k):
            if conv_q:
                _conv_piece(*conv_q.pop(0))

    def conv_flush(fams):
        while conv_q and conv_q[0][0] in fams:
            _conv_piece(*conv_q.pop(0))

    issue_ploads(0, extra=setup_done)
    issue_ploads(1)
    emit_p_sq(0)
    sp = emit_p_stats(0, d0=setup_done)
    emit_kraw(0, extra=setup_done)
    emit_p_reps(sp)
    _conv("ow")
    for j in range(n_ptile):
        t_a1_last, sp = emit_p_s2(sp)
    conv_flush({"ow"})
    t_t2 = pstate["t2"]; t_t3 = pstate["t3"]

    t_a1c = E("tensor", lambda e: [e.matmul(psBC[0:64, h:h + 512], c_zrow[0:1, :],
                                            kmB[0:1, 0:512], start=False, stop=True)
                                   for h in (0, 512)][-1],
              deps=[t_a1_last, t_zc])

    t_a1p = E("vector", lambda e: e.tensor_scalar_sub(a1p[:, :], psBC[0:64, 0:D],
                                                      t3acc[:, :]), deps=[t_a1c, t_t3])
    tr_toks = []
    prev_tr = None
    for a in range(NC):
        t_tr = E("tensor", lambda e, a=a: e.transpose(psH[:, 64:128],
                                                      a1p[:, a * 128:(a + 1) * 128],
                                                      c_identb[0:64, 0:64]),
                 deps=[t_a1p] + ([prev_tr] if prev_tr else []))
        t_ev = E("vector", lambda e, a=a: e.tensor_copy(a1pT[:, a, :], psH[:, 64:128]),
                 deps=[t_tr])
        prev_tr = t_ev
        tr_toks.append(t_ev)
    t_t2t = E("tensor", lambda e: e.transpose(psG[0:1, 0:64], t2acc[:, :],
                                              c_identf[0:64, 0:64]),
              deps=[t_t2, t_a1p])
    t_t2r2 = E("vector", lambda e: e.tensor_copy(c_t2row[:, :], psG[0:1, 0:64]), deps=[t_t2t])
    kv0 = []
    for h in (0, 512):
        t_mm = E("tensor", lambda e, h=h: [e.matmul(psBC[0:64, h:h + 512], a1pT[:, a, :],
                                                    vwp[:, a, h:h + 512], start=(a == 0),
                                                    stop=False) for a in range(NC)][-1],
                 deps=tr_toks + [t_fold, t_a1p])
        t_r1 = E("tensor", lambda e, h=h: e.matmul(psBC[0:64, h:h + 512], c_t2row[0:1, :],
                                                   c_cfv[0:1, h:h + 512], start=False,
                                                   stop=True), deps=[t_mm, t_t2r2, t_cfvr])
        kv0.append(t_r1)
    t_kvF = E("vector", lambda e: e.tensor_copy(kvF[:, :], psBC[0:64, 0:D]), deps=kv0)
    t_kvB = E("vector", lambda e: e.tensor_copy(kvB[:, :], kvF[:, :]), deps=[t_kvF])
    t_ksF = E("vector", lambda e: e.tensor_copy(ksF[:, :], t2acc[:, :]), deps=[t_t2])
    t_ksB = E("vector", lambda e: e.tensor_copy(ksB[:, :], ksF[:, :]), deps=[t_ksF])
    dbg_pend = []
    if dbg is not None:
        dbg_pend.append(E("sync", lambda e: [
            e.dma_start(out=dbg[:, 0:1], in_=ksF[:, :], allow_slow_non_contiguous=True),
            e.dma_start(out=dbg[:, 2:1026], in_=kvF[:, :])],
            deps=[t_ksF, t_kvF], n=2, sem="dbg"))

    esp.close()
    pref_done = sch.barrier()

    esa = ExitStack()
    Sa = lambda name, shape, dtyp=F32: esa.enter_context(nc.sbuf_tensor(name, shape, dtyp))
    XT0 = Sa("XT0", [128, NC, TILE]); XT1 = Sa("XT1", [128, NC, TILE])
    XB0 = Sa("XB0", [128, NC, TILE], BF16)
    ZB0 = Sa("ZB0", [128, NC, TILE], BF16); ZB1 = Sa("ZB1", [128, NC, TILE], BF16)
    VSB = Sa("VSB", [128, TILE // 128, D], BF16)
    ATT = Sa("ATT", [128, NC, TILE], BF16)
    OP0 = Sa("OP0", [128, NC, 128], BF16)
    OP1 = Sa("OP1", [128, NC, 128], BF16)
    OP2 = Sa("OP2", [128, NC, 128], BF16)

    use = {k: None for k in ("XT0", "XT1", "XB0", "ZB0", "ZB1", "VSB", "rows",
                             "ATT", "OP0", "OP1", "OP2",
                             "psA", "psBC", "psD", "psE", "psF", "psG", "psH",
                             "qt", "kt", "smt", "ktok", "repP", "repQ")}
    use["state"] = (t_kvB, t_ksB)

    def dep(*names):
        out = []
        for n in names:
            v = use[n]
            if v is None:
                continue
            if isinstance(v, tuple) and len(v) == 2 and isinstance(v[0], str):
                out.append(v)
            else:
                out.extend([x for x in v if x])
        return out

    atiles = [(0, HALO, True)] + [(HALO + k * TILE, TILE, False) for k in range(n_atile)]

    def emit_stage1_A(ti):
        off, C, halo = atiles[ti]
        d0 = pref_done if ti == 0 else []
        p = ti % 2
        XT = XT0 if p == 0 else XT1
        xtn = "XT0" if p == 0 else "XT1"
        xb = XB0
        xbn = "XB0"
        ZB = ZB0 if p == 0 else ZB1
        zbn = "ZB0" if p == 0 else "ZB1"
        t_x = E("sync", lambda e, off=off, C=C, XT=XT: [e.dma_start(
            out=XT[:, a, 0:C], in_=xm_f[:, a, off:off + C]) for a in range(NC)],
            deps=dep(xtn) + d0, n=NC, sem=xtn)
        t_xb = E("gdma", lambda e, off=off, C=C, xb=xb: [e.dma_start(
            out=xb[:, a, 0:C], in_=xm_f[:, a, off:off + C]) for a in range(NC)],
            deps=dep(xbn) + d0, n=NC, sem=xbn)
        st = dict(ti=ti, off=off, C=C, halo=halo, XT=XT, xtn=xtn, ZB=ZB, zbn=zbn,
                  t_x=t_x, t_xb=t_xb, xb=xb)
        t_sq = E("scalar", lambda e, C=C, xb=xb, ZB=ZB: e.activation(
            ZB[:, :, 0:C], xb[:, :, 0:C], AF.Square), deps=[t_xb] + dep(zbn) + d0)
        st["t_sq"] = t_sq
        return st

    def emit_s1A_b(st):
        C = st["C"]; ZB = st["ZB"]; xb = st["xb"]
        t_rows, t_psa = emit_stats(xb[:, :, 0:C], ZB[:, :, 0:C], C, [st["t_xb"]], [],
                                   dep("psA"), sq_tok=st["t_sq"],
                                   deps_rows=dep("rows"))
        use["psA"] = t_psa
        use["XB0"] = t_rows
        st["t_rows"] = t_rows

    def emit_s1A_c(st):
        C = st["C"]; XT = st["XT"]; ZB = st["ZB"]
        t_c1, t_c2 = emit_reps(repPm, repPr, C, [st["t_rows"]], dep("psA"),
                               dep("repP"))
        use["psA"] = t_c2
        use["rows"] = t_c2
        t_z1 = E("vector", lambda e, C=C, XT=XT, ZB=ZB: e.tensor_tensor(
            ZB[:, :, 0:C], XT[:, :, 0:C], _bc_rep(repPm, C), op=ALU.subtract),
            deps=[st["t_x"], t_c1])
        t_z2 = E("vector", lambda e, C=C, ZB=ZB: e.tensor_tensor(
            ZB[:, :, 0:C], ZB[:, :, 0:C], _bc_rep(repPr, C), op=ALU.mult),
            deps=[t_z1, t_c2])
        st["z_toks"] = [t_z2]
        use["repP"] = t_z2

    def emit_stage2_A(st, nxt=None):
        ti = st["ti"]; off = st["off"]; C = st["C"]; halo = st["halo"]
        XT = st["XT"]; ZB = st["ZB"]
        t_x = st["t_x"]; z_toks = st["z_toks"]
        nsub = C // 128
        d0 = pref_done if ti == 0 else []
        t_qmm = E("tensor", lambda e, C=C, ZB=ZB: [e.matmul(psD[0:64, 0:C], qwp[:, a, :],
                                                   ZB[:, a, 0:C], start=(a == 0),
                                                   stop=(a == NC - 1)) for a in range(NC)][-1],
                  deps=z_toks + dep("psD"))
        t_q1 = E("vector", lambda e, C=C: e.tensor_scalar(kA[:, 0:C], psD[0:64, 0:C],
                                                          c_cfq[:, :], 0.0, op0=ALU.add,
                                                          op1=ALU.max),
                 deps=[t_qmm, t_cfq] + dep("qt"))
        t_q2 = E("vector", lambda e, C=C: e.tensor_scalar(kB[:, 0:C], psD[0:64, 0:C],
                                                          c_cfq[:, :], 0.0, op0=ALU.add,
                                                          op1=ALU.min), deps=[t_qmm, t_cfq])
        t_q3 = E("scalar", lambda e, C=C: e.activation(kB[:, 0:C], kB[:, 0:C], AF.Exp),
                 deps=[t_q2])
        t_qt = E("vector", lambda e, C=C: e.tensor_tensor(qt_sb[:, 0:C], kA[:, 0:C],
                                                          kB[:, 0:C], op=ALU.add),
                 deps=[t_q1, t_q3])
        if nxt is not None:
            emit_s1A_b(nxt)
        t_kmm = E("tensor", lambda e, C=C, ZB=ZB: [e.matmul(psD[64:128, 0:C], kwp[:, a, :],
                                                   ZB[:, a, 0:C], start=(a == 0),
                                                   stop=(a == NC - 1)) for a in range(NC)][-1],
                  deps=z_toks + [t_q1, t_q2])
        t_k1e = E("vector", lambda e, C=C: e.tensor_scalar(kA[:, 0:C], psD[64:128, 0:C],
                                                           c_cfk[:, :], 0.0, op0=ALU.add,
                                                           op1=ALU.max),
                  deps=[t_kmm, t_cfk, t_qt])
        t_k2e = E("vector", lambda e, C=C: e.tensor_scalar(kB[:, 0:C], psD[64:128, 0:C],
                                                           c_cfk[:, :], 0.0, op0=ALU.add,
                                                           op1=ALU.min),
                  deps=[t_kmm, t_cfk, t_qt])
        t_k3 = E("scalar", lambda e, C=C: e.activation(kB[:, 0:C], kB[:, 0:C], AF.Exp),
                 deps=[t_k2e])
        t_kt = E("vector", lambda e, C=C: e.tensor_tensor(kt_sb[:, 0:C], kA[:, 0:C],
                                                          kB[:, 0:C], op=ALU.add),
                 deps=[t_k1e, t_k3] + dep("kt"))
        if halo:
            t_kt = E("vector", lambda e, C=C: e.tensor_tensor(kt_sb[:, 0:C], kt_sb[:, 0:C],
                                                              c_hmrep[:, 0:C], op=ALU.mult),
                     deps=[t_kt, t_hmr])
        use["psD"] = t_kt

        v_toks = []
        for s in range(nsub):
            for h in range(2):
                ps = psE if h == 0 else psF
                pn = "psE" if h == 0 else "psF"
                t_vm = E("tensor", lambda e, s=s, h=h, ps=ps, ZB=ZB: [
                    e.matmul(ps[:, 0:512], ZB[:, a, s * 128:(s + 1) * 128],
                             vwp[:, a, h * 512:(h + 1) * 512], start=(a == 0),
                             stop=(a == NC - 1)) for a in range(NC)][-1],
                    deps=z_toks + dep(pn))
                t_ve = E("vector", lambda e, s=s, h=h, ps=ps: e.tensor_tensor(
                    VSB[:, s, h * 512:(h + 1) * 512], ps[:, 0:512],
                    c_cfv_rep[:, h * 512:(h + 1) * 512], op=ALU.add),
                    deps=[t_vm, t_cfv] + (dep("VSB") if (s == 0 and h == 0) else []))
                use[pn] = t_ve
                v_toks.append(t_ve)
        use["VSB"] = v_toks[-1]
        t_vall = v_toks[-1]
        if nxt is not None:
            emit_s1A_c(nxt)

        at_last = None
        for s in range(nsub):
            sl = slice(s * 128, (s + 1) * 128)
            t_st = E("tensor", lambda e, sl=sl: e.matmul(psG[:, 0:128], kt_sb[:, sl],
                                                         qt_sb[:, sl], start=True, stop=True),
                     deps=[t_kt, t_qt] + dep("psG"))
            t_smt = E("vector", lambda e: e.tensor_tensor(smt_sb[:, :], psG[:, 0:128],
                                                          c_maskT[:, :], op=ALU.mult),
                      deps=[t_st, t_msk] + dep("smt"))
            t_dn = E("tensor", lambda e, sl=sl: [
                e.matmul(psG[0:1, 128:256], c_ones_b[:, :], smt_sb[:, :], start=True,
                         stop=False),
                e.matmul(psG[0:1, 128:256], ksB[:, :], qt_sb[:, sl], start=False,
                         stop=True)][-1],
                deps=[t_smt, use["state"][1], t_ones, t_qt])
            t_de = E("vector", lambda e: e.tensor_scalar_add(rd_row[0:1, 0:128],
                                                             psG[0:1, 128:256], ATTN_EPS),
                     deps=[t_dn])
            t_dl = E("scalar", lambda e: e.activation(rd_row[0:1, 0:128],
                                                      rd_row[0:1, 0:128], AF.Ln),
                     deps=[t_de])
            t_dr = E("scalar", lambda e: e.activation(rd_row[0:1, 0:128],
                                                      rd_row[0:1, 0:128], AF.Exp,
                                                      scale=-1.0), deps=[t_dl])
            num_toks = []
            for a in range(NC):
                ps = psE if a < 4 else psF
                pn = "psE" if a < 4 else "psF"
                rg = slice((a % 4) * 128, (a % 4) * 128 + 128)
                t_n = E("tensor", lambda e, a=a, s=s, sl=sl, ps=ps, rg=rg: [
                    e.matmul(ps[:, rg], kvB[:, a * 128:(a + 1) * 128], qt_sb[:, sl],
                             start=True, stop=False),
                    e.matmul(ps[:, rg], VSB[:, s, a * 128:(a + 1) * 128], smt_sb[:, :],
                             start=False, stop=True)][-1],
                    deps=[t_smt, use["state"][0], t_vall] + (dep(pn) if a % 4 == 0 else []))
                num_toks.append(t_n)
            t_db = E("tensor", lambda e: e.matmul(psG[:, 256:384], c_ones_f[0:1, :],
                                                  rd_row[0:1, 0:128], start=True, stop=True),
                     deps=[t_dr, t_onef])
            t_dbc = E("scalar", lambda e: e.copy(drep_sb[:, :], psG[:, 256:384]),
                      deps=[t_db])
            t_ktr = E("tensor", lambda e, sl=sl: e.transpose(psH[:, 0:64], kt_sb[:, sl],
                                                             c_identb[0:64, 0:64]),
                      deps=[t_kt, t_idb] + dep("psH"))
            t_kte = E("vector", lambda e: e.tensor_copy(ktok_sb[:, :], psH[:, 0:64]),
                      deps=[t_ktr] + dep("ktok"))
            use["psH"] = t_kte
            at_toks = []
            for h in range(2):
                ps = psE if h == 0 else psF
                t_at = E("vector", lambda e, h=h, s=s, ps=ps: e.tensor_tensor(
                    ATT[:, h * 4:h * 4 + 4, s * 128:(s + 1) * 128],
                    ps[:, 0:512].rearrange("p (b c) -> p b c", b=4),
                    _bc_rep(drep_sb, 128, n=4, w=128), op=ALU.mult),
                    deps=[num_toks[3 if h == 0 else 7], t_dbc] +
                         (dep("ATT") if (s == 0 and h == 0) else []))
                at_toks.append(t_at)
            use["psE"] = at_toks[0]
            use["psF"] = at_toks[1]
            use["psG"] = t_dbc
            t_kv = E("tensor", lambda e, s=s: [
                e.matmul(psBC[0:64, 0:512], ktok_sb[:, :], VSB[:, s, 0:512],
                         start=True, stop=True),
                e.matmul(psBC[0:64, 512:1024], ktok_sb[:, :], VSB[:, s, 512:1024],
                         start=True, stop=True),
                e.matmul(psBC[64:128, 0:1], ktok_sb[:, :], c_ones_b[:, :],
                         start=True, stop=True)][-1],
                deps=[t_kte, t_vall] + dep("psBC"))
            t_ksf = E("vector", lambda e: e.tensor_tensor(ksF[:, :], ksF[:, :],
                                                          psBC[64:128, 0:1], op=ALU.add),
                      deps=[t_kv, use["state"][1]] + dbg_pend)
            t_ksb = E("scalar", lambda e: e.copy(ksB[:, :], ksF[:, :]), deps=[t_ksf])
            t_kvf = E("vector", lambda e: e.tensor_tensor(kvF[:, :], kvF[:, :],
                                                          psBC[0:64, 0:D], op=ALU.add),
                      deps=[t_kv, use["state"][0]] + dbg_pend)
            t_kvb = E("scalar", lambda e: e.copy(kvB[:, :], kvF[:, :]), deps=[t_kvf])
            dbg_pend.clear()
            use["state"] = (t_kvb, t_ksb)
            use["psBC"] = t_ksf
            use["smt"] = num_toks[-1]
            use["ktok"] = t_kv
            at_last = at_toks[-1]

        o_toks = []
        for do in range(NC):
            op = (OP0, OP1, OP2)[do % 3]
            opn = ("OP0", "OP1", "OP2")[do % 3]
            ps = (psE, psF, psD, psG)[do % 4]
            pn = ("psE", "psF", "psD", "psG")[do % 4]
            t_ow = E("sdma", lambda e, do=do, op=op: e.dma_start(out=op[:, :, :],
                                                                in_=ow16[do]),
                     deps=dep(opn) + [cnv["ow"]]
                     + (d0 if ti == 0 and do < 3 else []), sem=opn)
            t_om = E("tensor", lambda e, C=C, op=op, ps=ps: [
                e.matmul(ps[:, 0:C], op[:, a, :], ATT[:, a, 0:C], start=(a == 0),
                         stop=(a == NC - 1)) for a in range(NC)][-1],
                deps=[t_ow, at_last] + dep(pn))
            t_oe = E("vector", lambda e, do=do, off=off, C=C, ps=ps, XT=XT:
                e.scalar_tensor_tensor(
                    sT[:, do, off:off + C], ps[:, 0:C], c_ob[:, do:do + 1],
                    XT[:, do, 0:C], op0=ALU.add, op1=ALU.add),
                deps=[t_om, t_x, tld])
            use[pn] = t_oe
            use[opn] = t_om
            o_toks.append(t_oe)
            conv_step(1)
        use[st["xtn"]] = o_toks[-1]
        use["ATT"] = o_toks[-1]
        use[st["zbn"]] = v_toks[-1]
        use["qt"] = o_toks[-1]
        use["kt"] = o_toks[-1]
        if dbg is not None and ti == 0:
            dbg_pend.append(E("sync", lambda e: [
                e.dma_start(out=dbg[:, 1:2], in_=ksF[:, :], allow_slow_non_contiguous=True),
                e.dma_start(out=dbg[:, 1026:2050], in_=kvF[:, :])],
                deps=[use["state"][0], use["state"][1]], n=2, sem="dbg"))

    _conv("f1")
    st_cur = emit_stage1_A(0)
    emit_s1A_b(st_cur)
    emit_s1A_c(st_cur)
    for ti in range(len(atiles)):
        st_next = emit_stage1_A(ti + 1) if ti + 1 < len(atiles) else None
        emit_stage2_A(st_cur, st_next)
        st_cur = st_next
    conv_flush({"f1"})

    esa.close()
    esPA.close()
    a_done = sch.barrier()

    esb = ExitStack()
    Sb = lambda name, shape, dtyp=F32: esb.enter_context(nc.sbuf_tensor(name, shape, dtyp))
    PC = Sb("PC", [128, NC, BMAX], BF16)
    H20 = Sb("H20", [128, NC, BMAX], BF16)
    H21 = Sb("H21", [128, NC, BMAX], BF16)
    H22 = Sb("H22", [128, NC, BMAX], BF16)
    G = Sb("G", [128, 32, BMAX], BF16)
    F1A = Sb("F1A", [128, 2, NC, 128], BF16)
    F1B = Sb("F1B", [128, 2, NC, 128], BF16)
    F2A = Sb("F2A", [128, 32, 128], BF16)
    F2B = Sb("F2B", [128, 32, 128], BF16)

    useB = {k: None for k in ("PC", "H20", "H21", "H22", "G", "Ghi", "F1A", "F1B",
                              "F2A", "F2B", "xacc", "rows",
                              "psA", "psBCa", "psBCb", "psD", "psE", "psF", "psG",
                              "repP", "repQ")}

    def depB(*names):
        return [useB[n] for n in names if useB[n] is not None]

    _prr = [0]
    PRING = [(psE, 0, "psE"), (psF, 0, "psF"), (psD, 0, "psD"),
             (psG, 0, "psG"), (psBC, 0, "psBCa"), (psBC, 512, "psBCb")]

    def wide_regions(BCW, rot):
        regs = []
        for (h0, h1) in _nsplit(BCW):
            ps, off, pn = PRING[_prr[0] % 6]
            _prr[0] += 1
            regs.append((h0, h1, ps[:, off:off + h1 - h0], pn))
        return regs

    def H2of(bi):
        return (H20, H21, H22)[bi % 3], ("H20", "H21", "H22")[bi % 3]

    def emit_s1B_a(bi):
        boff, BCW = btiles[bi]
        d0 = a_done if bi == 0 else []
        H2, hkey = H2of(bi)
        casts = []
        for (u0, u1) in _nsplit(BCW):
            t_c = E("vector", lambda e, boff=boff, u0=u0, u1=u1: e.tensor_copy(
                PC[:, :, u0:u1], sT[:, :, boff + u0:boff + u1]), deps=depB("PC") + d0)
            t_sq = E("scalar", lambda e, u0=u0, u1=u1, H2=H2: e.activation(
                H2[:, :, u0:u1], PC[:, :, u0:u1], AF.Square),
                deps=[t_c] + depB(hkey) + d0)
            casts.append((u0, u1, t_c, t_sq))
        return dict(bi=bi, boff=boff, BCW=BCW, H2=H2, hkey=hkey, casts=casts)

    def emit_s1B_b(st):
        H2 = st["H2"]
        rowst = []
        for (u0, u1, t_c, t_sq) in st["casts"]:
            cw = u1 - u0
            t_r, t_psa = emit_stats(PC[:, :, u0:u1], H2[:, :, u0:u1], cw, [t_c], [],
                                    depB("psA"), sq_tok=t_sq, deps_rows=depB("rows"),
                                    row_off=u0)
            useB["psA"] = t_psa
            useB["PC"] = t_r
            rowst.append((u0, u1, t_r))
        st["rowst"] = rowst

    def emit_s1B_c(st):
        boff = st["boff"]; H2 = st["H2"]
        h2_toks = []
        for (u0, u1, t_r) in st["rowst"]:
            cw = u1 - u0
            t_c1, t_c2 = emit_reps(repPm, repPr, cw, [t_r], depB("psA"),
                                   depB("repP"), row_off=u0)
            useB["psA"] = t_c2
            useB["rows"] = t_c2
            t_z1 = E("vector", lambda e, boff=boff, u0=u0, u1=u1, cw=cw, H2=H2:
                     e.tensor_tensor(H2[:, :, u0:u1], sT[:, :, boff + u0:boff + u1],
                                     _bc_rep(repPm, cw), op=ALU.subtract), deps=[t_c1])
            t_z2 = E("vector", lambda e, u0=u0, u1=u1, cw=cw, H2=H2: e.tensor_tensor(
                H2[:, :, u0:u1], H2[:, :, u0:u1], _bc_rep(repPr, cw),
                op=ALU.mult), deps=[t_z1, t_c2])
            t_z3 = E("vector", lambda e, u0=u0, u1=u1, cw=cw, H2=H2: e.tensor_tensor(
                H2[:, :, u0:u1], H2[:, :, u0:u1], _bc_col(c_ln2w, cw),
                op=ALU.mult), deps=[t_z2, tld])
            t_z4 = E("vector", lambda e, u0=u0, u1=u1, cw=cw, H2=H2: e.tensor_tensor(
                H2[:, :, u0:u1], H2[:, :, u0:u1], _bc_col(c_ln2b, cw),
                op=ALU.add), deps=[t_z3])
            h2_toks.append(t_z4)
            useB["repP"] = t_z2
        st["h2_toks"] = h2_toks

    def emit_f1(st, inter0=None, inter1=None):
        bi = st["bi"]; BCW = st["BCW"]; H2 = st["H2"]
        h2_toks = st["h2_toks"]; d0 = a_done if bi == 0 else []
        g_toks = []
        for fg in range(16):
            fp = (F1A, F1B)[fg % 2]
            fpn = ("F1A", "F1B")[fg % 2]
            t_w = E("gdma" if fg % 2 == 0 else "sync",
                    lambda e, fg=fg, fp=fp: e.dma_start(
                        out=fp[:, :, :, :], in_=f1s[fg]),
                    deps=depB(fpn) + [cnv["f1"]]
                    + (d0 if bi == 0 and fg < 2 else []), sem=fpn)
            for t in range(2):
                fo = 2 * fg + t
                regs = wide_regions(BCW, fo % 3)
                t_m = E("tensor", lambda e, t=t, fp=fp, regs=regs, H2=H2: [
                    e.matmul(reg, fp[:, t, a, :], H2[:, a, h0:h1], start=(a == 0),
                             stop=(a == NC - 1))
                    for (h0, h1, reg, _) in regs for a in range(NC)][-1],
                    deps=[t_w] + h2_toks + depB(*{r[3] for r in regs}))
                gdeps = [t_m] + (depB("G") if fo == 0 else []) \
                    + (depB("Ghi") if fo >= 24 else [])
                t_g = E("scalar", lambda e, fo=fo, regs=regs: [
                    e.activation(G[:, fo, h0:h1], reg, AF.Gelu, bias=c_f1b[:, fo:fo + 1],
                                 scale=1.0) for (h0, h1, reg, _) in regs][-1],
                    deps=gdeps)
                for (_, _, _, pn) in regs:
                    useB[pn] = t_g
                useB[fpn] = t_m
                g_toks.append(t_g)
            if fg == 0 and inter0 is not None:
                inter0()
            if fg == 1 and inter1 is not None:
                inter1()
            conv_step(1)
        useB[st["hkey"]] = g_toks[-1]
        st["g_toks"] = g_toks

    def emit_f2(st, inter0=None, inter1=None):
        bi = st["bi"]; boff = st["boff"]; BCW = st["BCW"]
        g_toks = st["g_toks"]; d0 = a_done if bi == 0 else []
        conv_flush({"f2"})
        s2_toks = []
        for do in range(NC):
            fp = F2A if do % 2 == 0 else F2B
            fpn = "F2A" if do % 2 == 0 else "F2B"
            t_w = E("gdma" if do % 2 == 0 else "sync",
                    lambda e, do=do, fp=fp: e.dma_start(out=fp[:, :, :],
                                                        in_=f2s[do]),
                    deps=depB(fpn) + [cnv["f2"]], sem=fpn)
            regs = wide_regions(BCW, do % 3)
            t_m = E("tensor", lambda e, regs=regs, fp=fp: [
                e.matmul(reg, fp[:, h, :], G[:, h, h0:h1], start=(h == 0), stop=(h == 31))
                for (h0, h1, reg, _) in regs for h in range(32)][-1],
                deps=[t_w] + g_toks + depB(*{r[3] for r in regs}))
            t_e = E("vector", lambda e, do=do, boff=boff, regs=regs: [
                e.scalar_tensor_tensor(sT[:, do, boff + h0:boff + h1], reg,
                                       c_f2b[:, do:do + 1], sT[:, do, boff + h0:boff + h1],
                                       op0=ALU.add, op1=ALU.add)
                for (h0, h1, reg, _) in regs][-1], deps=[t_m, tld])
            for (_, _, _, pn) in regs:
                useB[pn] = t_e
            useB[fpn] = t_m
            s2_toks.append(t_e)
            conv_step(1)
            if do == 0 and inter0 is not None:
                inter0()
            if do == 1 and inter1 is not None:
                inter1()
        useB["G"] = s2_toks[-1]

        t_p = E("gdma", lambda e, boff=boff, BCW=BCW: [e.dma_start(
            out=sT[:, a, boff:boff + BCW], in_=xm_f[:, a, boff:boff + BCW],
            accum_op=ALU.add) for a in range(NC)],
            deps=s2_toks + d0 + depB("xacc"), n=NC, sem="xacc")
        useB["xacc"] = t_p
        st["t_p"] = t_p

    def emit_mln1_a(st):
        boff = st["boff"]; BCW = st["BCW"]
        H2, hkey = H2of(st["bi"])
        mcasts = []
        for (u0, u1) in _nsplit(BCW):
            t_c = E("vector", lambda e, boff=boff, u0=u0, u1=u1: e.tensor_copy(
                G[:, 24:32, u0:u1], sT[:, :, boff + u0:boff + u1]),
                deps=[st["t_p"]] + depB("G"))
            t_sq = E("scalar", lambda e, u0=u0, u1=u1, H2=H2: e.activation(
                H2[:, :, u0:u1], G[:, 24:32, u0:u1], AF.Square),
                deps=[t_c] + depB(hkey))
            mcasts.append((u0, u1, t_c, t_sq))
        st["mcasts"] = mcasts

    def emit_mln1_b(st):
        H2, hkey = H2of(st["bi"])
        mrowst = []
        for (u0, u1, t_c, t_sq) in st["mcasts"]:
            cw = u1 - u0
            t_r, t_psa = emit_stats(G[:, 24:32, u0:u1], H2[:, :, u0:u1], cw, [t_c], [],
                                    depB("psA"), sq_tok=t_sq, deps_rows=depB("rows"),
                                    row_off=u0)
            useB["psA"] = t_psa
            useB["Ghi"] = t_r
            useB[hkey] = t_r
            mrowst.append((u0, u1, t_r))
        st["mrowst"] = mrowst

    def emit_mln1_c(st):
        bi = st["bi"]; boff = st["boff"]
        for (u0, u1, t_r) in st["mrowst"]:
            cw = u1 - u0
            t_c1, t_c2 = emit_reps(repQm, repQr, cw, [t_r], depB("psA"),
                                   depB("repQ"), row_off=u0)
            useB["psA"] = t_c2
            useB["rows"] = t_c2
            t_y1 = E("vector", lambda e, boff=boff, u0=u0, u1=u1, cw=cw: e.tensor_tensor(
                sT[:, :, boff + u0:boff + u1], sT[:, :, boff + u0:boff + u1],
                _bc_rep(repQm, cw), op=ALU.subtract), deps=[t_c1])
            t_y2 = E("vector", lambda e, boff=boff, u0=u0, u1=u1, cw=cw: e.tensor_tensor(
                sT[:, :, boff + u0:boff + u1], sT[:, :, boff + u0:boff + u1],
                _bc_rep(repQr, cw), op=ALU.mult), deps=[t_y1, t_c2])
            useB["repQ"] = t_y2
            if bi == 0 and u0 == 0:
                hs = E("vector", lambda e: e.tensor_scalar_mul(
                    sT[:, :, HALO - 1:HALO], sT[:, :, HALO - 1:HALO], c_hscale[:, :]),
                    deps=[t_y2, tld])
                useB["repQ"] = hs

    nt = len(btiles)
    _conv("f2")
    _conv("wr")
    _conv("wk")
    _conv("wv")
    sstate = {}
    sstate[0] = emit_s1B_a(0)
    emit_s1B_b(sstate[0])
    emit_s1B_c(sstate[0])
    if nt > 1:
        sstate[1] = emit_s1B_a(1)
        emit_s1B_b(sstate[1])

    def _mk_a(b):
        def f():
            sstate[b] = emit_s1B_a(b)
        return f

    for bi in range(nt):
        st = sstate[bi]
        i0 = (lambda b=bi: emit_s1B_c(sstate[b + 1])) if bi + 1 < nt else None
        i1 = (lambda b=bi: emit_mln1_b(sstate[b - 1])) if bi >= 1 else None
        emit_f1(st, i0, i1)
        i2 = (lambda b=bi: emit_mln1_c(sstate[b - 1])) if bi >= 1 else None
        i3 = _mk_a(bi + 2) if bi + 2 < nt else None
        emit_f2(st, i2, i3)
        emit_mln1_a(st)
        if bi + 2 < nt:
            emit_s1B_b(sstate[bi + 2])
    emit_mln1_b(sstate[nt - 1])
    emit_mln1_c(sstate[nt - 1])
    conv_flush({"f2", "wr", "wk", "wv"})

    esb.close()
    b_done = sch.barrier()

    esc = ExitStack()
    Sc = lambda name, shape, dtyp=F32: esc.enter_context(nc.sbuf_tensor(name, shape, dtyp))
    TD = Sc("TD", [128, NC, TILE], BF16)
    XRK = Sc("XRK", [128, NC, TILE], BF16)
    RSB = Sc("RSB", [128, NC, TILE], BF16)
    KAC = Sc("KAC", [128, NC, TILE], BF16)
    YS = Sc("YS", [128, NC, TILE])
    ktmp = Sc("ktmp", [128, 512])
    ktmp2 = Sc("ktmp2", [128, 512])
    PCc = Sc("PCc", [128, NC, TILE], BF16)
    SQc = Sc("SQc", [128, NC, TILE], BF16)
    XK2 = Sc("XK2", [128, NC, TILE], BF16)
    WP0 = Sc("WP0", [128, NC, 128], BF16)
    WP1 = Sc("WP1", [128, NC, 128], BF16)
    WP2 = Sc("WP2", [128, NC, 128], BF16)
    WP3 = Sc("WP3", [128, NC, 128], BF16)

    useC = {k: None for k in ("TD", "XRK", "RSB", "KAC", "YS", "PCc", "SQc",
                              "XK2", "rows", "WP0", "WP1", "WP2", "WP3",
                              "psA", "psBC", "psBCa", "psBCb", "psD", "psE", "psF",
                              "psG", "repP", "repQ")}

    def depC(*names):
        return [useC[n] for n in names if useC[n] is not None]

    out_toks = []
    ctiles = [(HALO + k * TILE, TILE) for k in range(T_OWN // TILE)]

    def emit_s1C(ci):
        coff, CW = ctiles[ci]
        d0 = b_done if ci == 0 else []
        t_td = E("vector", lambda e, coff=coff, CW=CW: e.tensor_tensor(
            TD[:, :, 0:CW], sT[:, :, coff:coff + CW], sT[:, :, coff - 1:coff + CW - 1],
            op=ALU.subtract), deps=depC("TD") + d0)
        t_xr1 = E("vector", lambda e, CW=CW: e.tensor_tensor(
            XRK[:, :, 0:CW], TD[:, :, 0:CW], _bc_col(c_mur, CW), op=ALU.mult),
            deps=[t_td, tld] + depC("XRK"))
        t_xr2 = E("vector", lambda e, coff=coff, CW=CW: e.tensor_tensor(
            XRK[:, :, 0:CW], XRK[:, :, 0:CW], sT[:, :, coff - 1:coff + CW - 1],
            op=ALU.add), deps=[t_xr1])
        return dict(ci=ci, coff=coff, CW=CW, td_toks=[t_td], xr_toks=[t_xr2])

    _crr = [0]
    CRING = [(psE, 0, "psE"), (psF, 0, "psF"), (psD, 0, "psD"),
             (psG, 0, "psG"), (psBC, 0, "psBCa"), (psBC, 512, "psBCb")]

    def emit_wmm(wdram, do, CW, rhs3, rhs_toks, extra=(), ck=None):
        wp = (WP0, WP1, WP2, WP3)[do % 4]
        wpn = ("WP0", "WP1", "WP2", "WP3")[do % 4]
        psb, off, pn = CRING[_crr[0] % 6]
        _crr[0] += 1
        ps = psb[:, off:off + 512]
        t_w = E("gdma" if do % 2 == 0 else "sdma",
                lambda e, do=do, wp=wp, wdram=wdram: e.dma_start(
                    out=wp[:, :, :], in_=wdram[do]),
                deps=depC(wpn) + [cnv[ck]] + list(extra), sem=wpn)
        t_m = E("tensor", lambda e, do=do, CW=CW, wp=wp, ps=ps, rhs3=rhs3: [
            e.matmul(ps[:, 0:CW], wp[:, a, do * 0:do * 0 + 128], rhs3[:, a, 0:CW],
                     start=(a == 0), stop=(a == NC - 1)) for a in range(NC)][-1],
            deps=[t_w] + rhs_toks + depC(pn))
        useC[wpn] = t_m
        return t_m, ps, pn

    def emit_xkst(st):
        coff = st["coff"]; CW = st["CW"]
        t_xk1 = E("vector", lambda e, CW=CW: e.tensor_tensor(
            XK2[:, :, 0:CW], TD[:, :, 0:CW], _bc_col(c_muk, CW), op=ALU.mult),
            deps=st["td_toks"] + [tld] + depC("XK2"))
        t_xk2 = E("vector", lambda e, coff=coff, CW=CW: e.tensor_tensor(
            XK2[:, :, 0:CW], XK2[:, :, 0:CW], sT[:, :, coff - 1:coff + CW - 1],
            op=ALU.add), deps=[t_xk1])
        st["xk_toks"] = [t_xk2]
        useC["TD"] = t_xk1

    def emit_mln2_a(st, ys_toks):
        CW = st["CW"]; mc = []
        for (u0, u1) in _nsplit(CW):
            t_c = E("vector", lambda e, u0=u0, u1=u1: e.tensor_copy(
                PCc[:, :, u0:u1], YS[:, :, u0:u1]), deps=ys_toks + depC("PCc"))
            t_sq = E("scalar", lambda e, u0=u0, u1=u1: e.activation(
                SQc[:, :, u0:u1], PCc[:, :, u0:u1], AF.Square),
                deps=[t_c] + depC("SQc"))
            mc.append((u0, u1, t_c, t_sq))
        st["mcasts"] = mc

    def emit_mln2_b(st):
        rows = []
        for (u0, u1, t_c, t_sq) in st["mcasts"]:
            cw = u1 - u0
            t_r, t_psa = emit_stats(PCc[:, :, u0:u1], SQc[:, :, u0:u1], cw, [t_c], [],
                                    depC("psA"), sq_tok=t_sq, deps_rows=depC("rows"),
                                    row_off=u0)
            useC["psA"] = t_psa
            useC["PCc"] = t_r
            useC["SQc"] = t_r
            rows.append((u0, u1, t_r))
        st["mrows"] = rows

    def emit_mln2_c(st):
        coff = st["coff"]; CW = st["CW"]
        fin_toks = []
        for (u0, u1, t_r) in st["mrows"]:
            cw = u1 - u0
            t_c1, t_c2 = emit_reps(repQm, repQr, cw, [t_r], depC("psA"),
                                   depC("repQ"), row_off=u0)
            useC["psA"] = t_c2
            useC["rows"] = t_c2
            t_1 = E("vector", lambda e, u0=u0, u1=u1, cw=cw: e.tensor_tensor(
                YS[:, :, u0:u1], YS[:, :, u0:u1], _bc_rep(repQm, cw),
                op=ALU.subtract), deps=[t_c1])
            t_2 = E("vector", lambda e, u0=u0, u1=u1, cw=cw: e.tensor_tensor(
                YS[:, :, u0:u1], YS[:, :, u0:u1], _bc_rep(repQr, cw),
                op=ALU.mult), deps=[t_1, t_c2])
            useC["repQ"] = t_2
            fin_toks.append(t_2)
        st["fin_toks"] = fin_toks

    def emit_out_chunk(st, a):
        coff = st["coff"]; CW = st["CW"]
        t = E("sync", lambda e, a=a, coff=coff, CW=CW: e.dma_start(
            out=y2f[:, a, coff - HALO:coff - HALO + CW], in_=YS[:, a, 0:CW]),
            deps=st["fin_toks"], sem="yout")
        out_toks.append(t)
        if a == NC - 1:
            useC["YS"] = t

    def emit_out(st):
        for a in range(NC):
            emit_out_chunk(st, a)

    nct = len(ctiles)
    stC = {0: emit_s1C(0)}
    mprev = None
    for ci in range(nct):
        st = stC[ci]
        coff = st["coff"]; CW = st["CW"]
        d0 = b_done if ci == 0 else []
        r_toks = []
        for do in range(NC):
            t_m, ps, pn = emit_wmm(wr16, do, CW, XRK, st["xr_toks"],
                                   extra=(d0 if ci == 0 and do < 4 else []), ck="wr")
            t_s = E("scalar", lambda e, do=do, CW=CW, ps=ps: e.activation(
                RSB[:, do, 0:CW], ps[:, 0:CW], AF.Sigmoid),
                deps=[t_m] + (depC("RSB") if do == 0 else []))
            useC[pn] = t_s
            r_toks.append(t_s)
            if do == 1 and mprev is not None:
                emit_mln2_b(mprev)
            if do == 2:
                emit_xkst(st)
            if do == 4 and mprev is not None:
                emit_mln2_c(mprev)
        useC["XRK"] = r_toks[-1]
        ka_toks = []
        prev_sq = {0: None, 1: None}
        for do in range(NC):
            kt = ktmp if do % 2 == 0 else ktmp2
            t_m, ps, pn = emit_wmm(wk16, do, CW, XK2, st["xk_toks"], ck="wk")
            t_r = E("vector", lambda e, CW=CW, ps=ps, kt=kt: e.tensor_scalar_max(
                kt[:, 0:CW], ps[:, 0:CW], 0.0),
                deps=[t_m] + ([prev_sq[do % 2]] if prev_sq[do % 2] else []))
            t_q = E("scalar", lambda e, do=do, CW=CW, kt=kt: e.activation(
                KAC[:, do, 0:CW], kt[:, 0:CW], AF.Square),
                deps=[t_r] + (depC("KAC") if do == 0 else []))
            prev_sq[do % 2] = t_q
            useC[pn] = t_q
            ka_toks.append(t_q)
            if do == 1 and ci + 1 < nct:
                stC[ci + 1] = emit_s1C(ci + 1)
            if mprev is not None:
                emit_out_chunk(mprev, do)
        useC["XK2"] = ka_toks[-1]
        ys_toks = []
        prev_ty = {0: prev_sq[0], 1: prev_sq[1]}
        for do in range(NC):
            kt = ktmp if do % 2 == 0 else ktmp2
            t_m, ps, pn = emit_wmm(wv16, do, CW, KAC, ka_toks, ck="wv")
            t_u = E("vector", lambda e, do=do, CW=CW, ps=ps, kt=kt: e.tensor_tensor(
                kt[:, 0:CW], ps[:, 0:CW], RSB[:, do, 0:CW], op=ALU.mult),
                deps=[t_m] + ([prev_ty[do % 2]] if prev_ty[do % 2] else []))
            t_y = E("vector", lambda e, do=do, coff=coff, CW=CW, kt=kt: e.tensor_tensor(
                YS[:, do, 0:CW], kt[:, 0:CW], sT[:, do, coff:coff + CW], op=ALU.add),
                deps=[t_u] + (depC("YS") if do == 0 else []))
            prev_ty[do % 2] = t_y
            useC[pn] = t_y
            ys_toks.append(t_y)
        useC["RSB"] = ys_toks[-1]
        useC["KAC"] = ys_toks[-1]
        emit_mln2_a(st, ys_toks)
        mprev = st
    emit_mln2_b(mprev)
    emit_mln2_c(mprev)
    emit_out(mprev)

    esc.close()
    E("vector", lambda e: e.memset(srow_sb[0:1, 0:1], 0.0), deps=[out_toks[-1]],
      inc=True)

    with nc.Block() as block:
        sch.finalize(block)
    esg.close()
    return nc


def _perm_fm(a2d, nchunk=NC):
    Dtot, T = a2d.shape
    return np.ascontiguousarray(a2d.reshape(nchunk, 128, T).transpose(1, 0, 2))


def _pieces(wT, nin, nout):
    return np.ascontiguousarray(wT.reshape(nin, 128, nout, 128).transpose(2, 1, 0, 3))


_NC_CACHE = {}


def get_nc(key, cfg):
    if key not in _NC_CACHE:
        _NC_CACHE[key] = build_program(cfg)
    return _NC_CACHE[key]


def make_inputs(cfg, x, w):
    T_OWN = cfg["T_OWN"]; HALO = cfg["HALO"]; PPREF = cfg["PPREF"]
    n_cores = cfg["N_CORES"]; n_slice = cfg["N_SLICE"]
    f32 = np.float32
    shared = {}
    shared["identf"] = np.eye(128, dtype=f32)
    shared["ln1w8"] = w["ln1_w"].reshape(NC, 128).T
    shared["ln1b8"] = w["ln1_b"].reshape(NC, 128).T
    shared["ln2w8"] = w["ln2_w"].reshape(NC, 128).T
    shared["ln2b8"] = w["ln2_b"].reshape(NC, 128).T
    shared["mur8"] = w["mu_r"].reshape(NC, 128).T
    shared["muk8"] = w["mu_k"].reshape(NC, 128).T
    shared["ob8"] = w["o_b"].reshape(NC, 128).T
    shared["f2b8"] = w["f2_b"].reshape(NC, 128).T
    shared["f1b32"] = w["f1_b"].reshape(32, 128).T
    shared["qb_row"] = w["q_b"].reshape(1, FD)
    shared["kb_row"] = w["k_b"].reshape(1, FD)
    shared["vb_row"] = w["v_b"].reshape(1, D)
    shared["qwT"] = _perm_fm(w["q_w"].T)
    shared["kwT"] = _perm_fm(w["k_w"].T)
    shared["vwT"] = _perm_fm(w["v_w"].T)
    shared["owP"] = _pieces(w["o_w"].T, NC, NC)
    shared["f1P"] = _pieces(w["f1_w"].T, NC, 32)
    shared["f2P"] = _pieces(w["f2_w"].T, 32, NC)
    shared["wrP"] = _pieces(w["W_r"].T, NC, NC)
    shared["wkP"] = _pieces(w["W_k"].T, NC, NC)
    shared["wvP"] = _pieces(w["W_v"].T, NC, NC)
    shared = {k: np.ascontiguousarray(v, dtype=f32) for k, v in shared.items()}

    in_maps = []
    for c in range(n_cores):
        seq = c // n_slice
        s = c % n_slice
        start = s * T_OWN
        m = dict(shared)
        xm = np.zeros((T_OWN + HALO, D), dtype=f32)
        lo = start - HALO
        if lo < 0:
            xm[HALO:] = x[seq, 0:start + T_OWN]
        else:
            xm[:] = x[seq, lo:start + T_OWN]
        m["xm_f"] = _perm_fm(np.ascontiguousarray(xm.T))
        pr = np.zeros((PPREF, D), dtype=f32)
        L = max(start - HALO, 0)
        pr[:L] = x[seq, :L]
        m["xp_tok"] = np.ascontiguousarray(pr.reshape(PPREF // 128, 128, D))
        m["xp_f"] = _perm_fm(np.ascontiguousarray(pr.T))
        pm = np.zeros((1, PPREF), dtype=f32)
        pm[0, :L] = 1.0
        m["pmask"] = pm
        m["hmaskr"] = np.full((1, HALO), 1.0 if s > 0 else 0.0, dtype=f32)
        m["hscale"] = np.full((128, 1), 1.0 if s > 0 else 0.0, dtype=f32)
        in_maps.append(m)
    return in_maps


def gather_out(cfg, results, x_shape):
    T_OWN = cfg["T_OWN"]; n_slice = cfg["N_SLICE"]
    B, N, _ = x_shape
    out = np.empty((B, N, D), dtype=np.float32)
    for c in range(cfg["N_CORES"]):
        seq, s = c // n_slice, c % n_slice
        y = results[c]["y2f"]
        out[seq, s * T_OWN:(s + 1) * T_OWN] = \
            np.asarray(y).transpose(1, 0, 2).reshape(D, T_OWN).T
    return out


def kernel(L0_attn_x, L0_attn_ln1_w, L0_attn_ln1_b, L0_attn_q_w, L0_attn_q_b,
           L0_attn_k_w, L0_attn_k_b, L0_attn_v_w, L0_attn_v_b, L0_attn_o_w,
           L0_attn_o_b, L0_attn_ln2_w, L0_attn_ln2_b, L0_attn_f1_w, L0_attn_f1_b,
           L0_attn_f2_w, L0_attn_f2_b, L0_ffn_W_r, L0_ffn_W_k, L0_ffn_W_v,
           L0_ffn_mu_r, L0_ffn_mu_k):
    cfg = dict(FULL_CFG)
    x = np.asarray(L0_attn_x, dtype=np.float32)
    w = dict(ln1_w=L0_attn_ln1_w, ln1_b=L0_attn_ln1_b, q_w=L0_attn_q_w, q_b=L0_attn_q_b,
             k_w=L0_attn_k_w, k_b=L0_attn_k_b, v_w=L0_attn_v_w, v_b=L0_attn_v_b,
             o_w=L0_attn_o_w, o_b=L0_attn_o_b, ln2_w=L0_attn_ln2_w, ln2_b=L0_attn_ln2_b,
             f1_w=L0_attn_f1_w, f1_b=L0_attn_f1_b, f2_w=L0_attn_f2_w, f2_b=L0_attn_f2_b,
             W_r=L0_ffn_W_r, W_k=L0_ffn_W_k, W_v=L0_ffn_W_v, mu_r=L0_ffn_mu_r,
             mu_k=L0_ffn_mu_k)
    w = {k: np.asarray(v, dtype=np.float32) for k, v in w.items()}
    in_maps = make_inputs(cfg, x, w)
    nc = get_nc("full", cfg)
    res = run_bass_kernel_spmd(nc, in_maps, core_ids=list(range(cfg["N_CORES"])))
    return gather_out(cfg, res.results, x.shape)


# version 38 (final)
# speedup vs baseline: 1.1387x; 1.1387x over previous
import numpy as np
from contextlib import ExitStack

import concourse.bass as bass
import concourse.dve_ops as dve_ops
import concourse.mybir as mybir
from concourse.bass_utils import run_bass_kernel_spmd

F32 = mybir.dt.float32
BF16 = mybir.dt.bfloat16
AF = mybir.ActivationFunctionType
ALU = mybir.AluOpType
AX = mybir.AxisListType

D = 1024
NC = 8
FD = 64
LN_EPS = 1e-5
ATTN_EPS = 1e-6

FULL_CFG = dict(T_OWN=2048, HALO=128, PPREF=6144, PTILE=768, TILE=512,
                N_CORES=8, N_SLICE=4)


class Sched:
    STREAMS = ("sync", "gdma", "gcomp", "tensor", "vector", "scalar")
    ENGINE_OF = {"sync": "sync", "gdma": "gpsimd", "gcomp": "gpsimd",
                 "tensor": "tensor", "vector": "vector", "scalar": "scalar"}

    def __init__(self, nc, es):
        self.nc = nc
        self.es = es
        self.seq = 0
        self.late_toks = set()
        self.ops = []
        self.sems = {}
        self.sem_stream = {}
        self.cnt = {}
        for s in ("gcomp", "tensor", "vector", "scalar"):
            self._new_sem(s, s)

    def _new_sem(self, name, stream):
        self.sems[name] = self.es.enter_context(self.nc.semaphore("sem_" + name))
        self.sem_stream[name] = stream
        self.cnt[name] = 0

    def emit(self, stream, fn, deps=(), inc=True, n=1, sem=None, late=False):
        dma = stream in ("sync", "gdma")
        if sem is None:
            assert not dma, "DMA ops must name a slot semaphore"
            sem = stream
        if sem not in self.sems:
            self._new_sem(sem, stream)
        assert self.sem_stream[sem] == stream
        amt = 16 if dma else 1
        deps = [d for d in deps if d is not None]
        for d in deps:
            assert isinstance(d, tuple) and len(d) == 2 and isinstance(d[0], str), \
                f"bad dep {d!r} on stream {stream} (op #{self.seq})"
        tok = None
        if inc:
            self.cnt[sem] += amt * (n if dma else 1)
            tok = (sem, self.cnt[sem])
        self.ops.append((self.seq, stream, sem, fn, deps, tok, n if dma else 1, amt))
        if late and tok is not None:
            self.late_toks.add(tok)
        self.seq += 1
        return tok

    def barrier(self):
        return [(s, c) for s, c in self.cnt.items() if c > 0]

    def finalize(self, block):
        engines = {}
        for op in self.ops:
            engines.setdefault(self.ENGINE_OF[op[1]], []).append(op)
        for eng_attr, ops in engines.items():
            my_streams = [s for s in self.STREAMS if self.ENGINE_OF[s] == eng_attr]

            def section(eng, ops=ops, my_streams=my_streams):
                waited = {}
                for _, stream, sem, fn, deps, tok, n, amt in ops:
                    for (ds, dv) in deps:
                        if waited.get(ds, 0) >= dv:
                            continue
                        eng.wait_ge(self.sems[ds], dv)
                        waited[ds] = dv
                    ret = fn(eng)
                    if tok is not None:
                        insts = ret if isinstance(ret, list) else [ret]
                        if stream in ("sync", "gdma"):
                            assert len(insts) == n, \
                                f"{stream} op declared n={n} but emitted {len(insts)}"
                            for inst in insts:
                                inst.then_inc(self.sems[sem], amt)
                        else:
                            assert insts[-1] is not None
                            insts[-1].then_inc(self.sems[sem], amt)
                return None

            getattr(block, eng_attr)(section)


def _bc_rep(t, C, n=NC, w=512):
    return bass.AP(t, 0, [[w, 128], [0, n], [1, C]])


def _bc_col(t, C, n=NC):
    return bass.AP(t, 0, [[n, 128], [1, n], [0, C]])


def _nsplit(c, step=512):
    out, p = [], 0
    while p < c:
        w = min(step, c - p)
        out.append((p, p + w))
        p += w
    return out


def build_program(cfg):
    T_OWN = cfg["T_OWN"]; HALO = cfg["HALO"]; PPREF = cfg["PPREF"]
    PTILE = cfg["PTILE"]; TILE = cfg["TILE"]
    T_LOC = T_OWN + HALO
    n_atile = T_OWN // TILE
    n_ptile = PPREF // PTILE
    btiles = []
    pos = HALO - 1
    first = True
    while pos < T_LOC:
        w = min(513 if first else 512, T_LOC - pos)
        btiles.append((pos, w))
        pos += w
        first = False
    BMAX = max(w for _, w in btiles)

    nc = bass.Bass("TRN2", target_bir_lowering=False)

    dti = {}
    def din(name, shape, dtype=F32):
        dti[name] = nc.dram_tensor(name, shape, dtype, kind="ExternalInput").ap()
        return dti[name]

    xm_f = din("xm_f", [128, NC, T_LOC])
    xp_tok = din("xp_tok", [PPREF // 128, 128, D])
    xp_f = din("xp_f", [128, NC, PPREF])
    pmask = din("pmask", [1, PPREF])
    hmaskr = din("hmaskr", [1, HALO])
    hscale = din("hscale", [128, 1])
    identf = din("identf", [128, 128])
    ln1w8 = din("ln1w8", [128, NC]); ln1b8 = din("ln1b8", [128, NC])
    ln2w8 = din("ln2w8", [128, NC]); ln2b8 = din("ln2b8", [128, NC])
    mur8 = din("mur8", [128, NC]); muk8 = din("muk8", [128, NC])
    ob8 = din("ob8", [128, NC]); f2b8 = din("f2b8", [128, NC])
    f1b32 = din("f1b32", [128, 32])
    qb_row = din("qb_row", [1, FD]); kb_row = din("kb_row", [1, FD])
    vb_row = din("vb_row", [1, D])
    qwT = din("qwT", [128, NC, FD])
    kwT = din("kwT", [128, NC, FD])
    vwT = din("vwT", [128, NC, D])
    owP = din("owP", [NC, 128, NC, 128])
    f1P = din("f1P", [32, 128, NC, 128])
    f2P = din("f2P", [NC, 128, 32, 128])
    wrP = din("wrP", [NC, 128, NC, 128])
    wkP = din("wkP", [NC, 128, NC, 128])
    wvP = din("wvP", [NC, 128, NC, 128])
    ow16 = nc.dram_tensor("ow16", [NC, 128, NC, 128], BF16, kind="Internal").ap()
    f1s = nc.dram_tensor("f1s", [16, 128, 2, NC, 128], BF16, kind="Internal").ap()
    f2s = nc.dram_tensor("f2s", [NC, 128, 32, 128], BF16, kind="Internal").ap()
    wr16 = nc.dram_tensor("wr16", [NC, 128, NC, 128], BF16, kind="Internal").ap()
    wk16 = nc.dram_tensor("wk16", [NC, 128, NC, 128], BF16, kind="Internal").ap()
    wv16 = nc.dram_tensor("wv16", [NC, 128, NC, 128], BF16, kind="Internal").ap()
    y2f = nc.dram_tensor("y2f", [128, NC, T_OWN], F32, kind="ExternalOutput").ap()
    dbg = nc.dram_tensor("dbg", [64, 2080], F32, kind="ExternalOutput").ap() \
        if cfg.get("DBG") else None

    esg = ExitStack()
    S = lambda name, shape, dtyp=F32: esg.enter_context(nc.sbuf_tensor(name, shape, dtyp))
    esPA = ExitStack()
    Spa = lambda name, shape, dtyp=F32: esPA.enter_context(nc.sbuf_tensor(name, shape, dtyp))

    sT = S("sT", [128, NC, T_LOC])
    kvF = S("kvF", [64, D]); kvB = S("kvB", [64, D], BF16)
    ksF = S("ksF", [64, 1]); ksB = S("ksB", [64, 1], BF16)
    c_identf = S("c_identf", [128, 128])
    c_identb = S("c_identb", [128, 128], BF16)
    c_maskT = S("c_maskT", [128, 128])
    c_ones_b = S("c_ones_b", [128, 1], BF16)
    c_ones_f = S("c_ones_f", [1, 128])
    c_zcol = S("c_zcol", [128, 1], BF16)
    c_zrow = S("c_zrow", [1, 64], BF16)
    c_ln1w = S("c_ln1w", [128, NC]); c_ln1b = S("c_ln1b", [128, NC])
    c_ln1b_b = S("c_ln1b_b", [128, NC], BF16)
    c_ln2w = S("c_ln2w", [128, NC]); c_ln2b = S("c_ln2b", [128, NC])
    c_mur = S("c_mur", [128, NC]); c_muk = S("c_muk", [128, NC])
    c_ob = S("c_ob", [128, NC]); c_f2b = S("c_f2b", [128, NC])
    c_f1b = S("c_f1b", [128, 32])
    c_qb = S("c_qb", [1, FD]); c_kb = S("c_kb", [1, FD])
    c_cfq = S("c_cfq", [64, 1]); c_cfk = S("c_cfk", [64, 1])
    c_cfv = S("c_cfv", [1, D])
    c_cfv_rep = S("c_cfv_rep", [128, D], BF16)
    c_negs = S("c_negs", [64, 1])
    c_hmrow = S("c_hmrow", [1, HALO])
    c_hscale = S("c_hscale", [128, 1])
    mean_row = S("mean_row", [1, 768])
    var_row = S("var_row", [1, 516])
    r2_row = S("r2_row", [1, 516])
    rstd_row = S("rstd_row", [1, 768])
    srow_sb = S("srow_sb", [1, 516])
    repPm = S("repPm", [128, 512]); repPr = S("repPr", [128, 512])
    repQm = S("repQm", [128, 512]); repQr = S("repQr", [128, 512])
    dcol = S("dcol", [128, 4])
    qwp = Spa("qwp", [128, NC, FD], BF16)
    kwp = Spa("kwp", [128, NC, FD], BF16)
    vwp = Spa("vwp", [128, NC, D], BF16)
    qt_sb = Spa("qt_sb", [64, TILE], BF16); kt_sb = Spa("kt_sb", [64, TILE], BF16)
    smt_sb = Spa("smt_sb", [128, 128], BF16)
    ktok_sb = Spa("ktok_sb", [128, 64], BF16)
    kA = Spa("kA", [64, 512]); kB = Spa("kB", [64, 512])
    rd_row = Spa("rd_row", [1, 128])
    drep_sb = Spa("drep_sb", [128, 128])
    c_hmrep = Spa("c_hmrep", [64, HALO])

    psA = esg.enter_context(nc.psum_tensor("psA", [128, 512], F32))
    psBC = esg.enter_context(nc.psum_tensor("psBC", [128, 1024], F32))
    psD = esg.enter_context(nc.psum_tensor("psD", [128, 512], F32))
    psE = esg.enter_context(nc.psum_tensor("psE", [128, 512], F32))
    psF = esg.enter_context(nc.psum_tensor("psF", [128, 512], F32))
    psG = esg.enter_context(nc.psum_tensor("psG", [128, 512], F32))
    psH = esg.enter_context(nc.psum_tensor("psH", [128, 1024], BF16))

    sch = Sched(nc, esg)
    E = sch.emit

    tl = []
    def ld(dst, src):
        tl.append(E("sync", lambda e, d=dst, s=src: e.dma_start(out=d, in_=s),
                    sem="ld_setup"))
    ld(c_identf[:, :], identf[:, :])
    ld(c_ln1w[:, :], ln1w8[:, :]); ld(c_ln1b[:, :], ln1b8[:, :])
    ld(c_ln2w[:, :], ln2w8[:, :]); ld(c_ln2b[:, :], ln2b8[:, :])
    ld(c_mur[:, :], mur8[:, :]); ld(c_muk[:, :], muk8[:, :])
    ld(c_ob[:, :], ob8[:, :]); ld(c_f2b[:, :], f2b8[:, :]); ld(c_f1b[:, :], f1b32[:, :])
    ld(c_qb[:, :], qb_row[:, :]); ld(c_kb[:, :], kb_row[:, :]); ld(c_cfv[:, :], vb_row[:, :])
    ld(c_hscale[:, :], hscale[:, :])
    t_hm = E("sync", lambda e: e.dma_start(out=c_hmrow[0:1, 0:HALO], in_=hmaskr[:, :]),
             sem="ld_setup")
    tld = t_hm
    E("gdma", lambda e: e.dma_start(out=qwp[:, :, :], in_=qwT[:, :, :]), sem="g_setup")
    E("gdma", lambda e: e.dma_start(out=kwp[:, :, :], in_=kwT[:, :, :]), sem="g_setup")
    E("gdma", lambda e: e.dma_start(out=vwp[:, :, :], in_=vwT[:, :, :]), sem="g_setup")
    t_idb = E("gdma", lambda e: e.dma_start(out=c_identb[:, :], in_=identf[:, :]),
              sem="g_setup")
    t_qw = t_kw = t_vw = t_idb
    t_ones = E("gcomp", lambda e: e.memset(c_ones_b[:, :], 1.0))
    t_onef = E("gcomp", lambda e: e.memset(c_ones_f[:, :], 1.0))
    t_zc = E("gcomp", lambda e: [e.memset(c_zcol[:, :], 0.0),
                                 e.memset(c_zrow[:, :], 0.0)][-1])

    t_io1 = E("gcomp", lambda e: e.iota(repPm[:, 0:128], [[1, 128]], channel_multiplier=0,
                                        allow_small_or_imprecise_dtypes=True))
    t_io2 = E("gcomp", lambda e: e.iota(repPm[:, 128:256], [[0, 128]], channel_multiplier=1,
                                        allow_small_or_imprecise_dtypes=True))
    t_msk = E("vector", lambda e: e.tensor_tensor(c_maskT[:, :], repPm[:, 128:256],
                                                  repPm[:, 0:128], op=ALU.is_le),
              deps=[t_io1, t_io2])

    t_l1b = E("vector", lambda e: e.tensor_copy(c_ln1b_b[:, :], c_ln1b[:, :]), deps=[tld])

    cf_chain = [None]
    def emit_cf(w_bf, dep_w, brow, width, col_out=None, row_out=None):
        outs = []
        prev = cf_chain[0]
        for (h0, h1) in _nsplit(width):
            cw = h1 - h0
            t_mm = E("tensor", lambda e, h0=h0, cw=cw, w_bf=w_bf: [
                e.matmul(psA[0:1, 0:cw], c_ln1b_b[:, a:a + 1], w_bf[:, a, h0:h0 + cw],
                         start=(a == 0), stop=(a == NC - 1)) for a in range(NC)][-1],
                     deps=[t_l1b, dep_w] + ([prev] if prev else []))
            t_add = E("vector", lambda e, h0=h0, cw=cw: e.tensor_tensor(
                (row_out if row_out is not None else srow_sb)[0:1, h0:h0 + cw],
                psA[0:1, 0:cw], brow[0:1, h0:h0 + cw], op=ALU.add), deps=[t_mm, tld])
            prev = t_add
            outs.append(t_add)
        if col_out is not None:
            t_tr = E("tensor", lambda e: e.matmul(
                psG[0:width, 0:1], (row_out if row_out is not None else srow_sb)[0:1, 0:width],
                c_ones_f[0:1, 0:1], start=True, stop=True), deps=outs + [t_onef])
            t_cp = E("vector", lambda e: e.tensor_copy(col_out[:, :], psG[0:width, 0:1]),
                     deps=[t_tr])
            cf_chain[0] = t_cp
            return t_cp
        cf_chain[0] = outs[-1]
        return outs[-1]

    t_cfq = emit_cf(qwp, t_qw, c_qb, FD, col_out=c_cfq)
    t_cfk = emit_cf(kwp, t_kw, c_kb, FD, col_out=c_cfk)
    t_cfvr = emit_cf(vwp, t_vw, c_cfv, D, row_out=c_cfv)

    fold_toks = []
    for a in range(NC):
        fold_toks.append(E("vector", lambda e, a=a: e.tensor_scalar_mul(
            qwp[:, a, :], qwp[:, a, :], c_ln1w[:, a:a + 1]), deps=[t_qw, tld, t_cfq]))
        fold_toks.append(E("vector", lambda e, a=a: e.tensor_scalar_mul(
            kwp[:, a, :], kwp[:, a, :], c_ln1w[:, a:a + 1]), deps=[t_kw, tld, t_cfk]))
        fold_toks.append(E("vector", lambda e, a=a: e.tensor_scalar_mul(
            vwp[:, a, :], vwp[:, a, :], c_ln1w[:, a:a + 1]), deps=[t_vw, tld, t_cfvr]))
    t_fold = fold_toks[-1]

    cfvb = []
    for h in range(0, D, 512):
        t_mm = E("tensor", lambda e, h=h: e.matmul(psBC[:, h:h + 512], c_ones_f[0:1, :],
                                                   c_cfv[0:1, h:h + 512], start=True, stop=True),
                 deps=[t_cfvr, t_onef])
        cfvb.append(E("vector", lambda e, h=h: e.tensor_copy(
            c_cfv_rep[:, h:h + 512], psBC[:, h:h + 512]), deps=[t_mm]))
    t_cfv = cfvb[-1]

    t_smm = E("tensor", lambda e: [e.matmul(psA[0:1, 0:FD], c_ones_b[:, 0:1], kwp[:, a, :],
                                            start=(a == 0), stop=(a == NC - 1))
                                   for a in range(NC)][-1],
              deps=[t_fold, t_ones, cf_chain[0]])
    t_sng = E("vector", lambda e: e.tensor_scalar_mul(srow_sb[0:1, 0:FD], psA[0:1, 0:FD],
                                                      -1.0), deps=[t_smm])
    t_str = E("tensor", lambda e: e.matmul(psG[0:FD, 0:1], srow_sb[0:1, 0:FD],
                                           c_ones_f[0:1, 0:1], start=True, stop=True),
              deps=[t_sng, t_onef, cf_chain[0]])
    t_negs = E("vector", lambda e: e.tensor_copy(c_negs[:, :], psG[0:FD, 0:1]), deps=[t_str])

    t_hmm = E("tensor", lambda e: e.matmul(psG[0:64, 128:128 + HALO], c_ones_f[0:1, 0:64],
                                           c_hmrow[0:1, 0:HALO], start=True, stop=True),
              deps=[t_hm, t_onef, t_negs])
    t_hmr = E("vector", lambda e: e.tensor_copy(c_hmrep[:, :], psG[0:64, 128:128 + HALO]),
              deps=[t_hmm])

    t_i2 = E("vector", lambda e: e.memset(t2acc[:, :], 0.0))
    t_i3 = E("vector", lambda e: e.memset(t3acc[:, :], 0.0))

    setup_done = sch.barrier()

    def emit_stats(xb3, sq3, C, deps_x, deps_sq_slot, deps_psA, sq_tok=None,
                   deps_rows=(), row_off=0, tmp_off=None, rows_eng="vector"):
        o = row_off
        ot = row_off if tmp_off is None else tmp_off
        if sq_tok is None:
            t_sq = E("scalar", lambda e: e.activation(sq3, xb3, AF.Square),
                     deps=deps_x + deps_sq_slot)
        else:
            t_sq = sq_tok
        t_s = E("tensor", lambda e: [e.matmul(psA[0:1, 0:C], c_ones_b[:, 0:1],
                                              xb3[:, a, :], start=(a == 0), stop=(a == NC - 1))
                                     for a in range(NC)][-1],
                deps=deps_x + [t_ones] + deps_psA)
        t_q = E("tensor", lambda e: [e.matmul(psA[32:33, 0:C], c_ones_b[:, 0:1],
                                              sq3[:, a, :], start=(a == 0), stop=(a == NC - 1))
                                     for a in range(NC)][-1], deps=[t_sq])
        if rows_eng == "scalar":
            t_m = E("scalar", lambda e: e.activation(mean_row[0:1, o:o + C],
                                                     psA[0:1, 0:C], AF.Copy,
                                                     scale=1.0 / D),
                    deps=[t_s, t_q] + list(deps_rows))
        else:
            t_m = E("vector", lambda e: e.tensor_scalar_mul(mean_row[0:1, o:o + C],
                                                            psA[0:1, 0:C], 1.0 / D),
                    deps=[t_s, t_q] + list(deps_rows))
        t_m2 = E("scalar", lambda e: e.activation(srow_sb[0:1, ot:ot + C],
                                                  mean_row[0:1, o:o + C],
                                                  AF.Square), deps=[t_m])
        if rows_eng == "scalar":
            t_v = E("scalar", lambda e: e.activation(var_row[0:1, ot:ot + C],
                                                     psA[32:33, 0:C], AF.Copy,
                                                     scale=1.0 / D),
                    deps=[t_q, t_m] + list(deps_rows))
        else:
            t_v = E("vector", lambda e: e.tensor_scalar_mul(var_row[0:1, ot:ot + C],
                                                            psA[32:33, 0:C], 1.0 / D),
                    deps=[t_q] + list(deps_rows))
        t_u = E("vector", lambda e: e.scalar_tensor_tensor(var_row[0:1, ot:ot + C],
                                                           var_row[0:1, ot:ot + C],
                                                           LN_EPS, srow_sb[0:1, ot:ot + C],
                                                           op0=ALU.add, op1=ALU.subtract),
                deps=[t_v, t_m2])
        t_l = E("scalar", lambda e: e.activation(r2_row[0:1, ot:ot + C], var_row[0:1, ot:ot + C],
                                                 AF.Ln), deps=[t_u])
        t_rstd = E("scalar", lambda e: e.activation(rstd_row[0:1, o:o + C],
                                                    r2_row[0:1, ot:ot + C],
                                                    AF.Exp, scale=-0.5), deps=[t_l])
        return t_rstd, t_v

    def emit_bcast(ps_region, row_ap, deps, parts=128):
        return E("tensor", lambda e: e.matmul(ps_region, c_ones_f[0:1, 0:parts], row_ap,
                                              start=True, stop=True), deps=deps + [t_onef])

    def emit_reps(rep_m, rep_r, C, deps, deps_psA, deps_bufs, row_off=0):
        o = row_off
        t_b1 = E("tensor", lambda e: e.matmul(psA[:, 0:C], c_ones_f[0:1, :],
                                              mean_row[0:1, o:o + C], start=True, stop=True),
                 deps=deps + deps_psA + [t_onef])
        t_c1 = E("scalar", lambda e: e.copy(rep_m[:, 0:C], psA[:, 0:C]),
                 deps=[t_b1] + deps_bufs)
        t_b2 = E("tensor", lambda e: e.matmul(psA[:, 0:C], c_ones_f[0:1, :],
                                              rstd_row[0:1, o:o + C], start=True, stop=True),
                 deps=deps + [t_c1])
        t_c2 = E("scalar", lambda e: e.copy(rep_r[:, 0:C], psA[:, 0:C]),
                 deps=[t_b2] + deps_bufs)
        return t_c1, t_c2

    esp = ExitStack()
    Sp = lambda name, shape, dtyp=F32: esp.enter_context(nc.sbuf_tensor(name, shape, dtyp))
    NSUBP = PTILE // 128
    xpfb0 = Sp("xpfb0", [128, NC, PTILE], BF16)
    xpfb1 = Sp("xpfb1", [128, NC, PTILE], BF16)
    psq = Sp("psq", [128, NC, PTILE], BF16)
    xtokb0 = Sp("xtokb0", [128, NSUBP, 1024], BF16)
    k1tokB = Sp("k1tokB", [128, NSUBP, 64], BF16)
    kAp = Sp("kAp", [64, PTILE]); kBp = Sp("kBp", [64, PTILE])
    kmB = Sp("kmB", [64, PTILE], BF16); k1B = Sp("k1B", [64, PTILE], BF16)
    c_t2row = Sp("c_t2row", [1, 64])
    t2acc = Sp("t2acc", [64, 1]); t3acc = Sp("t3acc", [64, 1])
    a1p = Sp("a1p", [64, D], BF16)
    a1pT = Sp("a1pT", [128, NC, 64], BF16)
    pm_sb = Sp("pm_sb", [1, PTILE])
    prepM0 = Sp("prepM0", [128, PTILE], BF16); prepR0 = Sp("prepR0", [128, PTILE])
    prepQ0 = Sp("prepQ0", [128, PTILE], BF16)

    pstate = {"xpf0": None, "xpf1": None, "repP0": None, "repP1": None,
              "repQ0": None, "repQ1": None, "xtok0": None, "xtok1": None,
              "xq_sq": None, "xq_tok": None, "pm": None, "rows": None,
              "psA": None, "repP": None, "repQ": None, "kscr": None,
              "psD": None, "psH": None, "t2": t_i2, "t3": t_i3}
    a1_open = [False]
    t_a1_last = None

    pload = {}

    def issue_ploads(j, extra=()):
        if j >= n_ptile:
            return
        xkey = ("xpf0", "xpf1")[j % 2]
        xpfb = (xpfb0, xpfb1)[j % 2]
        t_xf = E("gdma", lambda e, j=j, xpfb=xpfb: [e.dma_start(
            out=xpfb[:, a, :], in_=xp_f[:, a, j * PTILE:(j + 1) * PTILE])
            for a in range(NC)],
            deps=([pstate[xkey]] if pstate[xkey] else []) + list(extra), n=NC, sem=xkey)
        pload[j] = t_xf

    krw = {}

    def emit_kraw(j, extra=()):
        if j >= n_ptile:
            return
        xpfb = (xpfb0, xpfb1)[j % 2]
        xkey = ("xpf0", "xpf1")[j % 2]
        t = E("tensor", lambda e, xpfb=xpfb: [e.matmul(
            (psE if u0 == 0 else psF)[0:64, 0:u1 - u0], kwp[:, a, :],
            xpfb[:, a, u0:u1], start=(a == 0), stop=(a == NC - 1))
            for (u0, u1) in _nsplit(PTILE) for a in range(NC)][-1],
            deps=[pload[j], t_fold]
            + ([pstate["psD"]] if pstate["psD"] else []) + list(extra))
        pstate[xkey] = t
        krw[j] = t

    sqtoks = {}

    def emit_p_sq(j):
        if j >= n_ptile:
            return
        xpfb = (xpfb0, xpfb1)[j % 2]
        sql = []
        for (u0, u1) in _nsplit(PTILE):
            t_sq = E("scalar", lambda e, u0=u0, u1=u1, xpfb=xpfb: e.activation(
                psq[:, :, u0:u1], xpfb[:, :, u0:u1], AF.Square),
                deps=[pload[j]] + ([pstate["xq_sq"]] if pstate["xq_sq"] else []))
            sql.append(t_sq)
        sqtoks[j] = sql

    def emit_p_stats(j, d0=()):
        if j >= n_ptile:
            return None
        d0 = list(d0)
        t_pm = E("sync", lambda e, j=j: e.dma_start(out=pm_sb[0:1, 0:PTILE],
                                                    in_=pmask[:, j * PTILE:(j + 1) * PTILE]),
                 deps=([pstate["pm"]] if pstate["pm"] else []) + d0, sem="pm")
        xpfb = (xpfb0, xpfb1)[j % 2]
        t_xf = pload[j]
        spans = []
        prev_r = pstate["rows"]
        sql = sqtoks.pop(j)
        for si, (u0, u1) in enumerate(_nsplit(PTILE)):
            cw = u1 - u0
            t_rows, t_psa = emit_stats(xpfb[:, :, u0:u1], psq[:, :, u0:u1], cw,
                                       [t_xf] + d0, [],
                                       ([pstate["psA"]] if pstate["psA"] else []),
                                       sq_tok=sql[si],
                                       deps_rows=([prev_r] if prev_r else []),
                                       row_off=u0, tmp_off=0, rows_eng="scalar")
            pstate["xq_sq"] = t_rows
            pstate["psA"] = t_psa
            prev_r = t_rows
            spans.append((u0, u1, t_rows))
        return dict(j=j, t_pm=t_pm, t_xf=t_xf, xpfb=xpfb, spans=spans)

    def emit_p_reps(stp):
        t_pm = stp["t_pm"]
        rep_toks = []
        for (u0, u1, t_rows) in stp["spans"]:
            cw = u1 - u0
            t_mrr = E("vector", lambda e, u0=u0, u1=u1: e.tensor_tensor(
                mean_row[0:1, u0:u1], mean_row[0:1, u0:u1], rstd_row[0:1, u0:u1],
                op=ALU.mult), deps=[t_rows])
            t_b1 = E("tensor", lambda e, u0=u0, u1=u1, cw=cw: e.matmul(
                psA[:, 0:cw], c_ones_f[0:1, :], mean_row[0:1, u0:u1], start=True,
                stop=True),
                     deps=[t_mrr, t_onef] + ([pstate["psA"]] if pstate["psA"] else []))
            t_c1 = E("scalar", lambda e, u0=u0, u1=u1: e.copy(
                prepM0[:, u0:u1], psA[:, 0:u1 - u0]),
                     deps=[t_b1] + ([pstate["repP0"]] if pstate["repP0"] else []))
            t_b2 = E("tensor", lambda e, u0=u0, u1=u1, cw=cw: e.matmul(
                psA[:, 0:cw], c_ones_f[0:1, :], rstd_row[0:1, u0:u1], start=True,
                stop=True), deps=[t_rows, t_c1])
            t_c2 = E("scalar", lambda e, u0=u0, u1=u1: e.copy(
                prepR0[:, u0:u1], psA[:, 0:u1 - u0]),
                     deps=[t_b2] + ([pstate["repP0"]] if pstate["repP0"] else []))
            t_b3 = E("tensor", lambda e, u0=u0, cw=cw: e.matmul(
                psA[:, 0:cw], c_ones_f[0:1, :], pm_sb[0:1, u0:u0 + cw], start=True,
                stop=True), deps=[t_pm, t_c2])
            pstate["pm"] = t_b3
            t_c3 = E("scalar", lambda e, u0=u0, u1=u1: e.copy(
                prepQ0[:, u0:u1], psA[:, 0:u1 - u0]),
                     deps=[t_b3] + ([pstate["repQ0"]] if pstate["repQ0"] else []))
            pstate["psA"] = t_c3
            rep_toks.append((t_c1, t_c2, t_c3))
        pstate["rows"] = rep_toks[-1][1]
        stp["rep_toks"] = rep_toks

    def emit_p_s2(stp):
        j = stp["j"]
        xtb = xtokb0
        prepM = prepM0
        prepR = prepR0
        prepQ = prepQ0
        rkey = "repP0"
        qkey = "repQ0"
        t_c1 = stp["rep_toks"][-1][0]; t_c2 = stp["rep_toks"][-1][1]
        t_c3 = stp["rep_toks"][-1][2]
        all_reps = [t for trio in stp["rep_toks"] for t in trio]
        t_xt = E("gdma", lambda e, j=j, xtb=xtb: e.dma_start(
            out=xtb[:, :, :],
            in_=xp_tok[j * NSUBP:(j + 1) * NSUBP].rearrange("n p d -> p n d")),
            deps=([pstate["xq_tok"]] if pstate["xq_tok"] else []), sem="xtok")
        conv_step(1)
        t_kraw = krw[j]
        emit_p_sq(j + 1)
        t_a0 = E("vector", lambda e: e.tensor_tensor(kAp[:, 0:512], psE[0:64, 0:512],
                                                     prepR[0:64, 0:512], op=ALU.mult),
                 deps=[t_kraw] + all_reps + ([pstate["kscr"]] if pstate["kscr"] else []))
        t_a = E("vector", lambda e: e.tensor_tensor(kAp[:, 512:PTILE], psF[0:64, 0:PTILE - 512],
                                                    prepR[0:64, 512:PTILE], op=ALU.mult),
                deps=[t_a0]) if PTILE > 512 else t_a0
        t_z = E("vector", lambda e: e.scalar_tensor_tensor(kBp[:, 0:PTILE],
                                                           prepM[0:64, 0:PTILE],
                                                           c_negs[:, :], kAp[:, 0:PTILE],
                                                           op0=ALU.mult, op1=ALU.add),
                deps=[t_a, t_negs])
        t_rel = E("vector", lambda e: e.tensor_scalar(kAp[:, 0:PTILE], kBp[:, 0:PTILE],
                                                      c_cfk[:, :], 0.0, op0=ALU.add,
                                                      op1=ALU.max), deps=[t_z, t_cfk])
        t_mn = E("vector", lambda e: e.tensor_scalar(kmB[:, 0:PTILE], kBp[:, 0:PTILE],
                                                     c_cfk[:, :], 0.0, op0=ALU.add,
                                                     op1=ALU.min), deps=[t_z, t_cfk])
        t_ex = E("scalar", lambda e: e.activation(kmB[:, 0:PTILE], kmB[:, 0:PTILE], AF.Exp),
                 deps=[t_mn])
        t_K = E("vector", lambda e: e.tensor_tensor(kAp[:, 0:PTILE], kAp[:, 0:PTILE],
                                                    kmB[:, 0:PTILE], op=ALU.add),
                deps=[t_rel, t_ex])
        t_km = E("vector", lambda e: e.tensor_tensor(kmB[:, 0:PTILE], kAp[:, 0:PTILE],
                                                     prepQ[0:64, 0:PTILE], op=ALU.mult),
                 deps=[t_K, t_c3])
        t_k1 = E("vector", lambda e: e.tensor_tensor(k1B[:, 0:PTILE], kmB[:, 0:PTILE],
                                                     prepR[0:64, 0:PTILE], op=ALU.mult),
                 deps=[t_km])
        t_t2r = E("vector", lambda e: e.reduce_sum(kBp[:, 0:1], kmB[:, 0:PTILE], axis=AX.X),
                  deps=[t_km], late=True)
        t_t2 = E("vector", lambda e: e.tensor_tensor(t2acc[:, :], t2acc[:, :], kBp[:, 0:1],
                                                     op=ALU.add), deps=[t_t2r, pstate["t2"]])
        pstate["t2"] = t_t2
        t_t3m = E("vector", lambda e: e.tensor_tensor(kAp[:, 0:PTILE], kmB[:, 0:PTILE],
                                                      prepM[0:64, 0:PTILE], op=ALU.mult),
                  deps=[t_km, t_c1, t_t2])
        t_t3r = E("vector", lambda e: e.reduce_sum(kBp[:, 1:2], kAp[:, 0:PTILE], axis=AX.X),
                  deps=[t_t3m], late=True)
        t_t3 = E("vector", lambda e: e.tensor_tensor(t3acc[:, :], t3acc[:, :], kBp[:, 1:2],
                                                     op=ALU.add), deps=[t_t3r, pstate["t3"]])
        pstate["t3"] = t_t3
        pstate[rkey] = t_t3m
        pstate[qkey] = t_km
        pstate["kscr"] = t_t3r
        pstate["psD"] = t_a

        sp_next = emit_p_stats(j + 1)
        emit_kraw(j + 1)
        issue_ploads(j + 2)

        tr_toks = []
        for s in range(NSUBP):
            tr_toks.append(E("tensor", lambda e, s=s: e.transpose(
                psH[:, s * 64:(s + 1) * 64], k1B[:, s * 128:(s + 1) * 128],
                c_identb[0:64, 0:64]),
                deps=[t_k1, t_idb] + ([pstate["psH"]] if pstate["psH"] and s == 0 else [])))
        t_kev = E("vector", lambda e: e.tensor_copy(
            k1tokB[:, :, :], psH[:, 0:NSUBP * 64].rearrange("p (n f) -> p n f", n=NSUBP)),
            deps=tr_toks)
        pstate["psH"] = t_kev
        first = not a1_open[0]
        t_mm = E("tensor", lambda e, first=first, xtb=xtb: [
            e.matmul(psBC[0:64, h:h + 512], k1tokB[:, s, :], xtb[:, s, h:h + 512],
                     start=(first and s == 0), stop=False)
            for s in range(NSUBP) for h in (0, 512)][-1],
            deps=[t_kev, t_xt])
        a1_open[0] = True
        pstate["xq_tok"] = t_mm
        if sp_next is not None:
            emit_p_reps(sp_next)
        return t_mm, sp_next

    cnv = {}
    conv_q = []

    def _conv_piece(fam, i):
        if fam == "f1":
            cnv["f1"] = E("gdma", lambda e, i=i: e.dma_start(
                out=f1s[i], in_=f1P[2 * i:2 * i + 2].rearrange("t p a h -> p t a h")),
                sem="cnv_f1")
        else:
            dst, srcw = {"ow": (ow16, owP), "f2": (f2s, f2P), "wr": (wr16, wrP),
                         "wk": (wk16, wkP), "wv": (wv16, wvP)}[fam]
            cnv[fam] = E("gdma", lambda e, dst=dst, srcw=srcw, i=i: e.dma_start(
                out=dst[i], in_=srcw[i]), sem="cnv_" + fam)

    def _conv(fam):
        n = 16 if fam == "f1" else NC
        conv_q.extend((fam, i) for i in range(n))

    def conv_step(k=1):
        for _ in range(k):
            if conv_q:
                _conv_piece(*conv_q.pop(0))

    def conv_flush(fams):
        while conv_q and conv_q[0][0] in fams:
            _conv_piece(*conv_q.pop(0))

    issue_ploads(0, extra=setup_done)
    issue_ploads(1)
    emit_p_sq(0)
    sp = emit_p_stats(0, d0=setup_done)
    emit_kraw(0, extra=setup_done)
    emit_p_reps(sp)
    _conv("ow")
    for j in range(n_ptile):
        t_a1_last, sp = emit_p_s2(sp)
    conv_flush({"ow"})
    t_t2 = pstate["t2"]; t_t3 = pstate["t3"]

    t_a1c = E("tensor", lambda e: [e.matmul(psBC[0:64, h:h + 512], c_zrow[0:1, :],
                                            kmB[0:1, 0:512], start=False, stop=True)
                                   for h in (0, 512)][-1],
              deps=[t_a1_last, t_zc])

    t_a1p = E("vector", lambda e: e.tensor_scalar_sub(a1p[:, :], psBC[0:64, 0:D],
                                                      t3acc[:, :]), deps=[t_a1c, t_t3])
    tr_toks = []
    prev_tr = None
    for a in range(NC):
        t_tr = E("tensor", lambda e, a=a: e.transpose(psH[:, 64:128],
                                                      a1p[:, a * 128:(a + 1) * 128],
                                                      c_identb[0:64, 0:64]),
                 deps=[t_a1p] + ([prev_tr] if prev_tr else []))
        t_ev = E("vector", lambda e, a=a: e.tensor_copy(a1pT[:, a, :], psH[:, 64:128]),
                 deps=[t_tr])
        prev_tr = t_ev
        tr_toks.append(t_ev)
    t_t2t = E("tensor", lambda e: e.transpose(psG[0:1, 0:64], t2acc[:, :],
                                              c_identf[0:64, 0:64]),
              deps=[t_t2, t_a1p])
    t_t2r2 = E("vector", lambda e: e.tensor_copy(c_t2row[:, :], psG[0:1, 0:64]), deps=[t_t2t])
    kv0 = []
    for h in (0, 512):
        t_mm = E("tensor", lambda e, h=h: [e.matmul(psBC[0:64, h:h + 512], a1pT[:, a, :],
                                                    vwp[:, a, h:h + 512], start=(a == 0),
                                                    stop=False) for a in range(NC)][-1],
                 deps=tr_toks + [t_fold, t_a1p])
        t_r1 = E("tensor", lambda e, h=h: e.matmul(psBC[0:64, h:h + 512], c_t2row[0:1, :],
                                                   c_cfv[0:1, h:h + 512], start=False,
                                                   stop=True), deps=[t_mm, t_t2r2, t_cfvr])
        kv0.append(t_r1)
    t_kvF = E("vector", lambda e: e.tensor_copy(kvF[:, :], psBC[0:64, 0:D]), deps=kv0)
    t_kvB = E("vector", lambda e: e.tensor_copy(kvB[:, :], kvF[:, :]), deps=[t_kvF])
    t_ksF = E("vector", lambda e: e.tensor_copy(ksF[:, :], t2acc[:, :]), deps=[t_t2])
    t_ksB = E("vector", lambda e: e.tensor_copy(ksB[:, :], ksF[:, :]), deps=[t_ksF])
    dbg_pend = []
    if dbg is not None:
        dbg_pend.append(E("sync", lambda e: [
            e.dma_start(out=dbg[:, 0:1], in_=ksF[:, :], allow_slow_non_contiguous=True),
            e.dma_start(out=dbg[:, 2:1026], in_=kvF[:, :])],
            deps=[t_ksF, t_kvF], n=2, sem="dbg"))

    esp.close()
    pref_done = sch.barrier()

    esa = ExitStack()
    Sa = lambda name, shape, dtyp=F32: esa.enter_context(nc.sbuf_tensor(name, shape, dtyp))
    XT0 = Sa("XT0", [128, NC, TILE]); XT1 = Sa("XT1", [128, NC, TILE])
    XB0 = Sa("XB0", [128, NC, TILE], BF16)
    ZB0 = Sa("ZB0", [128, NC, TILE], BF16); ZB1 = Sa("ZB1", [128, NC, TILE], BF16)
    VSB = Sa("VSB", [128, TILE // 128, D], BF16)
    ATT = Sa("ATT", [128, NC, TILE], BF16)
    OP0 = Sa("OP0", [128, NC, 128], BF16)
    OP1 = Sa("OP1", [128, NC, 128], BF16)
    OP2 = Sa("OP2", [128, NC, 128], BF16)

    use = {k: None for k in ("XT0", "XT1", "XB0", "ZB0", "ZB1", "VSB", "rows",
                             "ATT", "OP0", "OP1", "OP2",
                             "psA", "psBC", "psD", "psE", "psF", "psG", "psH",
                             "qt", "kt", "smt", "ktok", "repP", "repQ")}
    use["state"] = (t_kvB, t_ksB)

    def dep(*names):
        out = []
        for n in names:
            v = use[n]
            if v is None:
                continue
            if isinstance(v, tuple) and len(v) == 2 and isinstance(v[0], str):
                out.append(v)
            else:
                out.extend([x for x in v if x])
        return out

    atiles = [(0, HALO, True)] + [(HALO + k * TILE, TILE, False) for k in range(n_atile)]

    def emit_stage1_A(ti):
        off, C, halo = atiles[ti]
        d0 = pref_done if ti == 0 else []
        p = ti % 2
        XT = XT0 if p == 0 else XT1
        xtn = "XT0" if p == 0 else "XT1"
        xb = XB0
        xbn = "XB0"
        ZB = ZB0 if p == 0 else ZB1
        zbn = "ZB0" if p == 0 else "ZB1"
        t_x = E("sync", lambda e, off=off, C=C, XT=XT: [e.dma_start(
            out=XT[:, a, 0:C], in_=xm_f[:, a, off:off + C]) for a in range(NC)],
            deps=dep(xtn) + d0, n=NC, sem=xtn)
        t_xb = E("gdma", lambda e, off=off, C=C, xb=xb: [e.dma_start(
            out=xb[:, a, 0:C], in_=xm_f[:, a, off:off + C]) for a in range(NC)],
            deps=dep(xbn) + d0, n=NC, sem=xbn)
        st = dict(ti=ti, off=off, C=C, halo=halo, XT=XT, xtn=xtn, ZB=ZB, zbn=zbn,
                  t_x=t_x, t_xb=t_xb, xb=xb)
        t_sq = E("scalar", lambda e, C=C, xb=xb, ZB=ZB: e.activation(
            ZB[:, :, 0:C], xb[:, :, 0:C], AF.Square), deps=[t_xb] + dep(zbn) + d0)
        st["t_sq"] = t_sq
        return st

    def emit_s1A_b(st):
        C = st["C"]; ZB = st["ZB"]; xb = st["xb"]
        t_rows, t_psa = emit_stats(xb[:, :, 0:C], ZB[:, :, 0:C], C, [st["t_xb"]], [],
                                   dep("psA"), sq_tok=st["t_sq"],
                                   deps_rows=dep("rows"))
        use["psA"] = t_psa
        use["XB0"] = t_rows
        st["t_rows"] = t_rows

    def emit_s1A_c(st):
        C = st["C"]; XT = st["XT"]; ZB = st["ZB"]
        t_c1, t_c2 = emit_reps(repPm, repPr, C, [st["t_rows"]], dep("psA"),
                               dep("repP"))
        use["psA"] = t_c2
        use["rows"] = t_c2
        t_z1 = E("vector", lambda e, C=C, XT=XT, ZB=ZB: e.tensor_tensor(
            ZB[:, :, 0:C], XT[:, :, 0:C], _bc_rep(repPm, C), op=ALU.subtract),
            deps=[st["t_x"], t_c1])
        t_z2 = E("vector", lambda e, C=C, ZB=ZB: e.tensor_tensor(
            ZB[:, :, 0:C], ZB[:, :, 0:C], _bc_rep(repPr, C), op=ALU.mult),
            deps=[t_z1, t_c2])
        st["z_toks"] = [t_z2]
        use["repP"] = t_z2

    def emit_stage2_A(st, nxt=None):
        ti = st["ti"]; off = st["off"]; C = st["C"]; halo = st["halo"]
        XT = st["XT"]; ZB = st["ZB"]
        t_x = st["t_x"]; z_toks = st["z_toks"]
        nsub = C // 128
        d0 = pref_done if ti == 0 else []
        t_qmm = E("tensor", lambda e, C=C, ZB=ZB: [e.matmul(psD[0:64, 0:C], qwp[:, a, :],
                                                   ZB[:, a, 0:C], start=(a == 0),
                                                   stop=(a == NC - 1)) for a in range(NC)][-1],
                  deps=z_toks + dep("psD"))
        t_q1 = E("vector", lambda e, C=C: e.tensor_scalar(kA[:, 0:C], psD[0:64, 0:C],
                                                          c_cfq[:, :], 0.0, op0=ALU.add,
                                                          op1=ALU.max),
                 deps=[t_qmm, t_cfq] + dep("qt"))
        t_q2 = E("vector", lambda e, C=C: e.tensor_scalar(kB[:, 0:C], psD[0:64, 0:C],
                                                          c_cfq[:, :], 0.0, op0=ALU.add,
                                                          op1=ALU.min), deps=[t_qmm, t_cfq])
        t_q3 = E("scalar", lambda e, C=C: e.activation(kB[:, 0:C], kB[:, 0:C], AF.Exp),
                 deps=[t_q2])
        t_qt = E("vector", lambda e, C=C: e.tensor_tensor(qt_sb[:, 0:C], kA[:, 0:C],
                                                          kB[:, 0:C], op=ALU.add),
                 deps=[t_q1, t_q3])
        if nxt is not None:
            emit_s1A_b(nxt)
        t_kmm = E("tensor", lambda e, C=C, ZB=ZB: [e.matmul(psD[64:128, 0:C], kwp[:, a, :],
                                                   ZB[:, a, 0:C], start=(a == 0),
                                                   stop=(a == NC - 1)) for a in range(NC)][-1],
                  deps=z_toks + [t_q1, t_q2])
        t_k1e = E("vector", lambda e, C=C: e.tensor_scalar(kA[:, 0:C], psD[64:128, 0:C],
                                                           c_cfk[:, :], 0.0, op0=ALU.add,
                                                           op1=ALU.max),
                  deps=[t_kmm, t_cfk, t_qt])
        t_k2e = E("vector", lambda e, C=C: e.tensor_scalar(kB[:, 0:C], psD[64:128, 0:C],
                                                           c_cfk[:, :], 0.0, op0=ALU.add,
                                                           op1=ALU.min),
                  deps=[t_kmm, t_cfk, t_qt])
        t_k3 = E("scalar", lambda e, C=C: e.activation(kB[:, 0:C], kB[:, 0:C], AF.Exp),
                 deps=[t_k2e])
        t_kt = E("vector", lambda e, C=C: e.tensor_tensor(kt_sb[:, 0:C], kA[:, 0:C],
                                                          kB[:, 0:C], op=ALU.add),
                 deps=[t_k1e, t_k3] + dep("kt"))
        if halo:
            t_kt = E("vector", lambda e, C=C: e.tensor_tensor(kt_sb[:, 0:C], kt_sb[:, 0:C],
                                                              c_hmrep[:, 0:C], op=ALU.mult),
                     deps=[t_kt, t_hmr])
        use["psD"] = t_kt

        v_toks = []
        for s in range(nsub):
            for h in range(2):
                ps = psE if h == 0 else psF
                pn = "psE" if h == 0 else "psF"
                t_vm = E("tensor", lambda e, s=s, h=h, ps=ps, ZB=ZB: [
                    e.matmul(ps[:, 0:512], ZB[:, a, s * 128:(s + 1) * 128],
                             vwp[:, a, h * 512:(h + 1) * 512], start=(a == 0),
                             stop=(a == NC - 1)) for a in range(NC)][-1],
                    deps=z_toks + dep(pn))
                t_ve = E("vector", lambda e, s=s, h=h, ps=ps: e.tensor_tensor(
                    VSB[:, s, h * 512:(h + 1) * 512], ps[:, 0:512],
                    c_cfv_rep[:, h * 512:(h + 1) * 512], op=ALU.add),
                    deps=[t_vm, t_cfv] + (dep("VSB") if (s == 0 and h == 0) else []))
                use[pn] = t_ve
                v_toks.append(t_ve)
        use["VSB"] = v_toks[-1]
        t_vall = v_toks[-1]
        if nxt is not None:
            emit_s1A_c(nxt)

        at_last = None
        for s in range(nsub):
            sl = slice(s * 128, (s + 1) * 128)
            t_st = E("tensor", lambda e, sl=sl: e.matmul(psG[:, 0:128], kt_sb[:, sl],
                                                         qt_sb[:, sl], start=True, stop=True),
                     deps=[t_kt, t_qt] + dep("psG"))
            t_smt = E("vector", lambda e: e.tensor_tensor(smt_sb[:, :], psG[:, 0:128],
                                                          c_maskT[:, :], op=ALU.mult),
                      deps=[t_st, t_msk] + dep("smt"))
            t_dn = E("tensor", lambda e, sl=sl: [
                e.matmul(psG[0:1, 128:256], c_ones_b[:, :], smt_sb[:, :], start=True,
                         stop=False),
                e.matmul(psG[0:1, 128:256], ksB[:, :], qt_sb[:, sl], start=False,
                         stop=True)][-1],
                deps=[t_smt, use["state"][1], t_ones, t_qt])
            t_de = E("vector", lambda e: e.tensor_scalar_add(rd_row[0:1, 0:128],
                                                             psG[0:1, 128:256], ATTN_EPS),
                     deps=[t_dn])
            t_dl = E("scalar", lambda e: e.activation(rd_row[0:1, 0:128],
                                                      rd_row[0:1, 0:128], AF.Ln),
                     deps=[t_de])
            t_dr = E("scalar", lambda e: e.activation(rd_row[0:1, 0:128],
                                                      rd_row[0:1, 0:128], AF.Exp,
                                                      scale=-1.0), deps=[t_dl])
            num_toks = []
            for a in range(NC):
                ps = psE if a < 4 else psF
                pn = "psE" if a < 4 else "psF"
                rg = slice((a % 4) * 128, (a % 4) * 128 + 128)
                t_n = E("tensor", lambda e, a=a, s=s, sl=sl, ps=ps, rg=rg: [
                    e.matmul(ps[:, rg], kvB[:, a * 128:(a + 1) * 128], qt_sb[:, sl],
                             start=True, stop=False),
                    e.matmul(ps[:, rg], VSB[:, s, a * 128:(a + 1) * 128], smt_sb[:, :],
                             start=False, stop=True)][-1],
                    deps=[t_smt, use["state"][0], t_vall] + (dep(pn) if a % 4 == 0 else []))
                num_toks.append(t_n)
            t_db = E("tensor", lambda e: e.matmul(psG[:, 256:384], c_ones_f[0:1, :],
                                                  rd_row[0:1, 0:128], start=True, stop=True),
                     deps=[t_dr, t_onef])
            t_dbc = E("scalar", lambda e: e.copy(drep_sb[:, :], psG[:, 256:384]),
                      deps=[t_db])
            t_ktr = E("tensor", lambda e, sl=sl: e.transpose(psH[:, 0:64], kt_sb[:, sl],
                                                             c_identb[0:64, 0:64]),
                      deps=[t_kt, t_idb] + dep("psH"))
            t_kte = E("vector", lambda e: e.tensor_copy(ktok_sb[:, :], psH[:, 0:64]),
                      deps=[t_ktr] + dep("ktok"))
            use["psH"] = t_kte
            at_toks = []
            for h in range(2):
                ps = psE if h == 0 else psF
                t_at = E("vector", lambda e, h=h, s=s, ps=ps: e.tensor_tensor(
                    ATT[:, h * 4:h * 4 + 4, s * 128:(s + 1) * 128],
                    ps[:, 0:512].rearrange("p (b c) -> p b c", b=4),
                    _bc_rep(drep_sb, 128, n=4, w=128), op=ALU.mult),
                    deps=[num_toks[3 if h == 0 else 7], t_dbc] +
                         (dep("ATT") if (s == 0 and h == 0) else []))
                at_toks.append(t_at)
            use["psE"] = at_toks[0]
            use["psF"] = at_toks[1]
            use["psG"] = t_dbc
            t_kv = E("tensor", lambda e, s=s: [
                e.matmul(psBC[0:64, 0:512], ktok_sb[:, :], VSB[:, s, 0:512],
                         start=True, stop=True),
                e.matmul(psBC[0:64, 512:1024], ktok_sb[:, :], VSB[:, s, 512:1024],
                         start=True, stop=True),
                e.matmul(psBC[64:128, 0:1], ktok_sb[:, :], c_ones_b[:, :],
                         start=True, stop=True)][-1],
                deps=[t_kte, t_vall] + dep("psBC"))
            t_ksf = E("vector", lambda e: e.tensor_tensor(ksF[:, :], ksF[:, :],
                                                          psBC[64:128, 0:1], op=ALU.add),
                      deps=[t_kv, use["state"][1]] + dbg_pend)
            t_ksb = E("scalar", lambda e: e.copy(ksB[:, :], ksF[:, :]), deps=[t_ksf])
            t_kvf = E("vector", lambda e: e.tensor_tensor(kvF[:, :], kvF[:, :],
                                                          psBC[0:64, 0:D], op=ALU.add),
                      deps=[t_kv, use["state"][0]] + dbg_pend)
            t_kvb = E("scalar", lambda e: e.copy(kvB[:, :], kvF[:, :]), deps=[t_kvf])
            dbg_pend.clear()
            use["state"] = (t_kvb, t_ksb)
            use["psBC"] = t_ksf
            use["smt"] = num_toks[-1]
            use["ktok"] = t_kv
            at_last = at_toks[-1]

        o_toks = []
        for do in range(NC):
            op = (OP0, OP1, OP2)[do % 3]
            opn = ("OP0", "OP1", "OP2")[do % 3]
            ps = (psE, psF, psD, psG)[do % 4]
            pn = ("psE", "psF", "psD", "psG")[do % 4]
            t_ow = E("sync", lambda e, do=do, op=op: e.dma_start(out=op[:, :, :],
                                                                in_=ow16[do]),
                     deps=dep(opn) + [cnv["ow"]]
                     + (d0 if ti == 0 and do < 3 else []), sem=opn)
            t_om = E("tensor", lambda e, C=C, op=op, ps=ps: [
                e.matmul(ps[:, 0:C], op[:, a, :], ATT[:, a, 0:C], start=(a == 0),
                         stop=(a == NC - 1)) for a in range(NC)][-1],
                deps=[t_ow, at_last] + dep(pn))
            t_oe = E("vector", lambda e, do=do, off=off, C=C, ps=ps, XT=XT:
                e.scalar_tensor_tensor(
                    sT[:, do, off:off + C], ps[:, 0:C], c_ob[:, do:do + 1],
                    XT[:, do, 0:C], op0=ALU.add, op1=ALU.add),
                deps=[t_om, t_x, tld])
            use[pn] = t_oe
            use[opn] = t_om
            o_toks.append(t_oe)
            conv_step(1)
        use[st["xtn"]] = o_toks[-1]
        use["ATT"] = o_toks[-1]
        use[st["zbn"]] = v_toks[-1]
        use["qt"] = o_toks[-1]
        use["kt"] = o_toks[-1]
        if dbg is not None and ti == 0:
            dbg_pend.append(E("sync", lambda e: [
                e.dma_start(out=dbg[:, 1:2], in_=ksF[:, :], allow_slow_non_contiguous=True),
                e.dma_start(out=dbg[:, 1026:2050], in_=kvF[:, :])],
                deps=[use["state"][0], use["state"][1]], n=2, sem="dbg"))

    _conv("f1")
    st_cur = emit_stage1_A(0)
    emit_s1A_b(st_cur)
    emit_s1A_c(st_cur)
    for ti in range(len(atiles)):
        st_next = emit_stage1_A(ti + 1) if ti + 1 < len(atiles) else None
        emit_stage2_A(st_cur, st_next)
        st_cur = st_next
    conv_flush({"f1"})

    esa.close()
    esPA.close()
    a_done = sch.barrier()

    esb = ExitStack()
    Sb = lambda name, shape, dtyp=F32: esb.enter_context(nc.sbuf_tensor(name, shape, dtyp))
    PC = Sb("PC", [128, NC, BMAX], BF16)
    H20 = Sb("H20", [128, NC, BMAX], BF16)
    H21 = Sb("H21", [128, NC, BMAX], BF16)
    H22 = Sb("H22", [128, NC, BMAX], BF16)
    G = Sb("G", [128, 32, BMAX], BF16)
    F1A = Sb("F1A", [128, 2, NC, 128], BF16)
    F1B = Sb("F1B", [128, 2, NC, 128], BF16)
    F1C = Sb("F1C", [128, 2, NC, 128], BF16)
    F2A = Sb("F2A", [128, 32, 128], BF16)
    F2B = Sb("F2B", [128, 32, 128], BF16)
    F2C = Sb("F2C", [128, 32, 128], BF16)

    useB = {k: None for k in ("PC", "H20", "H21", "H22", "G", "Ghi",
                              "F1A", "F1B", "F1C", "F2A", "F2B", "F2C",
                              "xacc", "rows",
                              "psA", "psBCa", "psBCb", "psD", "psE", "psF", "psG",
                              "repP", "repQ")}

    def depB(*names):
        return [useB[n] for n in names if useB[n] is not None]

    _prr = [0]
    PRING = [(psE, 0, "psE"), (psF, 0, "psF"), (psD, 0, "psD"),
             (psG, 0, "psG"), (psBC, 0, "psBCa"), (psBC, 512, "psBCb")]

    def wide_regions(BCW, rot):
        regs = []
        for (h0, h1) in _nsplit(BCW):
            ps, off, pn = PRING[_prr[0] % 6]
            _prr[0] += 1
            regs.append((h0, h1, ps[:, off:off + h1 - h0], pn))
        return regs

    def H2of(bi):
        return (H20, H21, H22)[bi % 3], ("H20", "H21", "H22")[bi % 3]

    def emit_s1B_a(bi):
        boff, BCW = btiles[bi]
        d0 = a_done if bi == 0 else []
        H2, hkey = H2of(bi)
        casts = []
        for (u0, u1) in _nsplit(BCW):
            t_c = E("vector", lambda e, boff=boff, u0=u0, u1=u1: e.tensor_copy(
                PC[:, :, u0:u1], sT[:, :, boff + u0:boff + u1]), deps=depB("PC") + d0)
            t_sq = E("scalar", lambda e, u0=u0, u1=u1, H2=H2: e.activation(
                H2[:, :, u0:u1], PC[:, :, u0:u1], AF.Square),
                deps=[t_c] + depB(hkey) + d0)
            casts.append((u0, u1, t_c, t_sq))
        return dict(bi=bi, boff=boff, BCW=BCW, H2=H2, hkey=hkey, casts=casts)

    def emit_s1B_b(st):
        H2 = st["H2"]
        rowst = []
        for (u0, u1, t_c, t_sq) in st["casts"]:
            cw = u1 - u0
            t_r, t_psa = emit_stats(PC[:, :, u0:u1], H2[:, :, u0:u1], cw, [t_c], [],
                                    depB("psA"), sq_tok=t_sq, deps_rows=depB("rows"),
                                    row_off=u0)
            useB["psA"] = t_psa
            useB["PC"] = t_r
            rowst.append((u0, u1, t_r))
        st["rowst"] = rowst

    def emit_s1B_c(st):
        boff = st["boff"]; H2 = st["H2"]
        h2_toks = []
        for (u0, u1, t_r) in st["rowst"]:
            cw = u1 - u0
            t_c1, t_c2 = emit_reps(repPm, repPr, cw, [t_r], depB("psA"),
                                   depB("repP"), row_off=u0)
            useB["psA"] = t_c2
            useB["rows"] = t_c2
            t_z1 = E("vector", lambda e, boff=boff, u0=u0, u1=u1, cw=cw, H2=H2:
                     e.tensor_tensor(H2[:, :, u0:u1], sT[:, :, boff + u0:boff + u1],
                                     _bc_rep(repPm, cw), op=ALU.subtract), deps=[t_c1])
            t_z2 = E("vector", lambda e, u0=u0, u1=u1, cw=cw, H2=H2: e.tensor_tensor(
                H2[:, :, u0:u1], H2[:, :, u0:u1], _bc_rep(repPr, cw),
                op=ALU.mult), deps=[t_z1, t_c2])
            t_z3 = E("vector", lambda e, u0=u0, u1=u1, cw=cw, H2=H2: e.tensor_tensor(
                H2[:, :, u0:u1], H2[:, :, u0:u1], _bc_col(c_ln2w, cw),
                op=ALU.mult), deps=[t_z2, tld])
            t_z4 = E("vector", lambda e, u0=u0, u1=u1, cw=cw, H2=H2: e.tensor_tensor(
                H2[:, :, u0:u1], H2[:, :, u0:u1], _bc_col(c_ln2b, cw),
                op=ALU.add), deps=[t_z3])
            h2_toks.append(t_z4)
            useB["repP"] = t_z2
        st["h2_toks"] = h2_toks

    def emit_f1(st, inter0=None, inter1=None):
        bi = st["bi"]; BCW = st["BCW"]; H2 = st["H2"]
        h2_toks = st["h2_toks"]; d0 = a_done if bi == 0 else []
        g_toks = []
        for fg in range(16):
            fp = (F1A, F1B, F1C)[fg % 3]
            fpn = ("F1A", "F1B", "F1C")[fg % 3]
            t_w = E("gdma" if fg % 3 != 1 else "sync",
                    lambda e, fg=fg, fp=fp: e.dma_start(
                        out=fp[:, :, :, :], in_=f1s[fg]),
                    deps=depB(fpn) + [cnv["f1"]]
                    + (d0 if bi == 0 and fg < 3 else []), sem=fpn)
            for t in range(2):
                fo = 2 * fg + t
                regs = wide_regions(BCW, fo % 3)
                t_m = E("tensor", lambda e, t=t, fp=fp, regs=regs, H2=H2: [
                    e.matmul(reg, fp[:, t, a, :], H2[:, a, h0:h1], start=(a == 0),
                             stop=(a == NC - 1))
                    for (h0, h1, reg, _) in regs for a in range(NC)][-1],
                    deps=[t_w] + h2_toks + depB(*{r[3] for r in regs}))
                gdeps = [t_m] + (depB("G") if fo == 0 else []) \
                    + (depB("Ghi") if fo >= 24 else [])
                t_g = E("scalar", lambda e, fo=fo, regs=regs: [
                    e.activation(G[:, fo, h0:h1], reg, AF.Gelu, bias=c_f1b[:, fo:fo + 1],
                                 scale=1.0) for (h0, h1, reg, _) in regs][-1],
                    deps=gdeps)
                for (_, _, _, pn) in regs:
                    useB[pn] = t_g
                useB[fpn] = t_m
                g_toks.append(t_g)
            if fg == 0 and inter0 is not None:
                inter0()
            if fg == 1 and inter1 is not None:
                inter1()
            conv_step(1)
        useB[st["hkey"]] = g_toks[-1]
        st["g_toks"] = g_toks

    def emit_f2(st, inter0=None, inter1=None):
        bi = st["bi"]; boff = st["boff"]; BCW = st["BCW"]
        g_toks = st["g_toks"]; d0 = a_done if bi == 0 else []
        conv_flush({"f2"})
        s2_toks = []
        for do in range(NC):
            fp = (F2A, F2B, F2C)[do % 3]
            fpn = ("F2A", "F2B", "F2C")[do % 3]
            t_w = E("gdma" if do % 3 != 1 else "sync",
                    lambda e, do=do, fp=fp: e.dma_start(out=fp[:, :, :],
                                                        in_=f2s[do]),
                    deps=depB(fpn) + [cnv["f2"]], sem=fpn)
            regs = wide_regions(BCW, do % 3)
            t_m = E("tensor", lambda e, regs=regs, fp=fp: [
                e.matmul(reg, fp[:, h, :], G[:, h, h0:h1], start=(h == 0), stop=(h == 31))
                for (h0, h1, reg, _) in regs for h in range(32)][-1],
                deps=[t_w] + g_toks + depB(*{r[3] for r in regs}))
            t_e = E("vector", lambda e, do=do, boff=boff, regs=regs: [
                e.scalar_tensor_tensor(sT[:, do, boff + h0:boff + h1], reg,
                                       c_f2b[:, do:do + 1], sT[:, do, boff + h0:boff + h1],
                                       op0=ALU.add, op1=ALU.add)
                for (h0, h1, reg, _) in regs][-1], deps=[t_m, tld])
            for (_, _, _, pn) in regs:
                useB[pn] = t_e
            useB[fpn] = t_m
            s2_toks.append(t_e)
            conv_step(1)
            if do == 0 and inter0 is not None:
                inter0()
            if do == 1 and inter1 is not None:
                inter1()
        useB["G"] = s2_toks[-1]

        t_p = E("gdma", lambda e, boff=boff, BCW=BCW: [e.dma_start(
            out=sT[:, a, boff:boff + BCW], in_=xm_f[:, a, boff:boff + BCW],
            accum_op=ALU.add) for a in range(NC)],
            deps=s2_toks + d0 + depB("xacc"), n=NC, sem="xacc")
        useB["xacc"] = t_p
        st["t_p"] = t_p

    def emit_mln1_a(st):
        boff = st["boff"]; BCW = st["BCW"]
        H2, hkey = H2of(st["bi"])
        mcasts = []
        for (u0, u1) in _nsplit(BCW):
            t_c = E("vector", lambda e, boff=boff, u0=u0, u1=u1: e.tensor_copy(
                G[:, 24:32, u0:u1], sT[:, :, boff + u0:boff + u1]),
                deps=[st["t_p"]] + depB("G"))
            t_sq = E("scalar", lambda e, u0=u0, u1=u1, H2=H2: e.activation(
                H2[:, :, u0:u1], G[:, 24:32, u0:u1], AF.Square),
                deps=[t_c] + depB(hkey))
            mcasts.append((u0, u1, t_c, t_sq))
        st["mcasts"] = mcasts

    def emit_mln1_b(st):
        H2, hkey = H2of(st["bi"])
        mrowst = []
        for (u0, u1, t_c, t_sq) in st["mcasts"]:
            cw = u1 - u0
            t_r, t_psa = emit_stats(G[:, 24:32, u0:u1], H2[:, :, u0:u1], cw, [t_c], [],
                                    depB("psA"), sq_tok=t_sq, deps_rows=depB("rows"),
                                    row_off=u0)
            useB["psA"] = t_psa
            useB["Ghi"] = t_r
            useB[hkey] = t_r
            mrowst.append((u0, u1, t_r))
        st["mrowst"] = mrowst

    def emit_mln1_c(st):
        bi = st["bi"]; boff = st["boff"]
        for (u0, u1, t_r) in st["mrowst"]:
            cw = u1 - u0
            t_c1, t_c2 = emit_reps(repQm, repQr, cw, [t_r], depB("psA"),
                                   depB("repQ"), row_off=u0)
            useB["psA"] = t_c2
            useB["rows"] = t_c2
            t_y1 = E("vector", lambda e, boff=boff, u0=u0, u1=u1, cw=cw: e.tensor_tensor(
                sT[:, :, boff + u0:boff + u1], sT[:, :, boff + u0:boff + u1],
                _bc_rep(repQm, cw), op=ALU.subtract), deps=[t_c1])
            t_y2 = E("vector", lambda e, boff=boff, u0=u0, u1=u1, cw=cw: e.tensor_tensor(
                sT[:, :, boff + u0:boff + u1], sT[:, :, boff + u0:boff + u1],
                _bc_rep(repQr, cw), op=ALU.mult), deps=[t_y1, t_c2])
            useB["repQ"] = t_y2
            if bi == 0 and u0 == 0:
                hs = E("vector", lambda e: e.tensor_scalar_mul(
                    sT[:, :, HALO - 1:HALO], sT[:, :, HALO - 1:HALO], c_hscale[:, :]),
                    deps=[t_y2, tld])
                useB["repQ"] = hs

    nt = len(btiles)
    _conv("f2")
    _conv("wr")
    _conv("wk")
    _conv("wv")
    sstate = {}
    sstate[0] = emit_s1B_a(0)
    emit_s1B_b(sstate[0])
    emit_s1B_c(sstate[0])
    if nt > 1:
        sstate[1] = emit_s1B_a(1)
        emit_s1B_b(sstate[1])

    def _mk_a(b):
        def f():
            sstate[b] = emit_s1B_a(b)
        return f

    for bi in range(nt):
        st = sstate[bi]
        i0 = (lambda b=bi: emit_s1B_c(sstate[b + 1])) if bi + 1 < nt else None
        i1 = (lambda b=bi: emit_mln1_b(sstate[b - 1])) if bi >= 1 else None
        emit_f1(st, i0, i1)
        i2 = (lambda b=bi: emit_mln1_c(sstate[b - 1])) if bi >= 1 else None
        i3 = _mk_a(bi + 2) if bi + 2 < nt else None
        emit_f2(st, i2, i3)
        emit_mln1_a(st)
        if bi + 2 < nt:
            emit_s1B_b(sstate[bi + 2])
    emit_mln1_b(sstate[nt - 1])
    emit_mln1_c(sstate[nt - 1])
    conv_flush({"f2", "wr", "wk", "wv"})

    esb.close()
    b_done = sch.barrier()

    esc = ExitStack()
    Sc = lambda name, shape, dtyp=F32: esc.enter_context(nc.sbuf_tensor(name, shape, dtyp))
    TD = Sc("TD", [128, NC, TILE], BF16)
    XRK = Sc("XRK", [128, NC, TILE], BF16)
    RSB = Sc("RSB", [128, NC, TILE], BF16)
    KAC = Sc("KAC", [128, NC, TILE], BF16)
    YS = Sc("YS", [128, NC, TILE])
    ktmp = Sc("ktmp", [128, 512])
    ktmp2 = Sc("ktmp2", [128, 512])
    PCc = Sc("PCc", [128, NC, TILE], BF16)
    SQc = Sc("SQc", [128, NC, TILE], BF16)
    XK2 = Sc("XK2", [128, NC, TILE], BF16)
    WP0 = Sc("WP0", [128, NC, 128], BF16)
    WP1 = Sc("WP1", [128, NC, 128], BF16)
    WP2 = Sc("WP2", [128, NC, 128], BF16)
    WP3 = Sc("WP3", [128, NC, 128], BF16)

    useC = {k: None for k in ("TD", "XRK", "RSB", "KAC", "YS", "PCc", "SQc",
                              "XK2", "rows", "WP0", "WP1", "WP2", "WP3",
                              "psA", "psBC", "psBCa", "psBCb", "psD", "psE", "psF",
                              "psG", "repP", "repQ")}

    def depC(*names):
        return [useC[n] for n in names if useC[n] is not None]

    out_toks = []
    ctiles = [(HALO + k * TILE, TILE) for k in range(T_OWN // TILE)]

    def emit_s1C(ci):
        coff, CW = ctiles[ci]
        d0 = b_done if ci == 0 else []
        t_td = E("vector", lambda e, coff=coff, CW=CW: e.tensor_tensor(
            TD[:, :, 0:CW], sT[:, :, coff:coff + CW], sT[:, :, coff - 1:coff + CW - 1],
            op=ALU.subtract), deps=depC("TD") + d0)
        t_xr1 = E("vector", lambda e, CW=CW: e.tensor_tensor(
            XRK[:, :, 0:CW], TD[:, :, 0:CW], _bc_col(c_mur, CW), op=ALU.mult),
            deps=[t_td, tld] + depC("XRK"))
        t_xr2 = E("vector", lambda e, coff=coff, CW=CW: e.tensor_tensor(
            XRK[:, :, 0:CW], XRK[:, :, 0:CW], sT[:, :, coff - 1:coff + CW - 1],
            op=ALU.add), deps=[t_xr1])
        return dict(ci=ci, coff=coff, CW=CW, td_toks=[t_td], xr_toks=[t_xr2])

    _crr = [0]
    CRING = [(psE, 0, "psE"), (psF, 0, "psF"), (psD, 0, "psD"),
             (psG, 0, "psG"), (psBC, 0, "psBCa"), (psBC, 512, "psBCb")]

    def emit_wmm(wdram, do, CW, rhs3, rhs_toks, extra=(), ck=None):
        wp = (WP0, WP1, WP2, WP3)[do % 4]
        wpn = ("WP0", "WP1", "WP2", "WP3")[do % 4]
        psb, off, pn = CRING[_crr[0] % 6]
        _crr[0] += 1
        ps = psb[:, off:off + 512]
        t_w = E("gdma" if do % 2 == 0 else "sync",
                lambda e, do=do, wp=wp, wdram=wdram: e.dma_start(
                    out=wp[:, :, :], in_=wdram[do]),
                deps=depC(wpn) + [cnv[ck]] + list(extra), sem=wpn)
        t_m = E("tensor", lambda e, do=do, CW=CW, wp=wp, ps=ps, rhs3=rhs3: [
            e.matmul(ps[:, 0:CW], wp[:, a, do * 0:do * 0 + 128], rhs3[:, a, 0:CW],
                     start=(a == 0), stop=(a == NC - 1)) for a in range(NC)][-1],
            deps=[t_w] + rhs_toks + depC(pn))
        useC[wpn] = t_m
        return t_m, ps, pn

    def emit_xkst(st):
        coff = st["coff"]; CW = st["CW"]
        t_xk1 = E("vector", lambda e, CW=CW: e.tensor_tensor(
            XK2[:, :, 0:CW], TD[:, :, 0:CW], _bc_col(c_muk, CW), op=ALU.mult),
            deps=st["td_toks"] + [tld] + depC("XK2"))
        t_xk2 = E("vector", lambda e, coff=coff, CW=CW: e.tensor_tensor(
            XK2[:, :, 0:CW], XK2[:, :, 0:CW], sT[:, :, coff - 1:coff + CW - 1],
            op=ALU.add), deps=[t_xk1])
        st["xk_toks"] = [t_xk2]
        useC["TD"] = t_xk1

    def emit_mln2_a(st, ys_toks):
        CW = st["CW"]; mc = []
        for (u0, u1) in _nsplit(CW):
            t_c = E("vector", lambda e, u0=u0, u1=u1: e.tensor_copy(
                PCc[:, :, u0:u1], YS[:, :, u0:u1]), deps=ys_toks + depC("PCc"))
            t_sq = E("scalar", lambda e, u0=u0, u1=u1: e.activation(
                SQc[:, :, u0:u1], PCc[:, :, u0:u1], AF.Square),
                deps=[t_c] + depC("SQc"))
            mc.append((u0, u1, t_c, t_sq))
        st["mcasts"] = mc

    def emit_mln2_b(st):
        rows = []
        for (u0, u1, t_c, t_sq) in st["mcasts"]:
            cw = u1 - u0
            t_r, t_psa = emit_stats(PCc[:, :, u0:u1], SQc[:, :, u0:u1], cw, [t_c], [],
                                    depC("psA"), sq_tok=t_sq, deps_rows=depC("rows"),
                                    row_off=u0)
            useC["psA"] = t_psa
            useC["PCc"] = t_r
            useC["SQc"] = t_r
            rows.append((u0, u1, t_r))
        st["mrows"] = rows

    def emit_mln2_c(st):
        coff = st["coff"]; CW = st["CW"]
        fin_toks = []
        for (u0, u1, t_r) in st["mrows"]:
            cw = u1 - u0
            t_c1, t_c2 = emit_reps(repQm, repQr, cw, [t_r], depC("psA"),
                                   depC("repQ"), row_off=u0)
            useC["psA"] = t_c2
            useC["rows"] = t_c2
            t_1 = E("vector", lambda e, u0=u0, u1=u1, cw=cw: e.tensor_tensor(
                YS[:, :, u0:u1], YS[:, :, u0:u1], _bc_rep(repQm, cw),
                op=ALU.subtract), deps=[t_c1])
            t_2 = E("vector", lambda e, u0=u0, u1=u1, cw=cw: e.tensor_tensor(
                YS[:, :, u0:u1], YS[:, :, u0:u1], _bc_rep(repQr, cw),
                op=ALU.mult), deps=[t_1, t_c2])
            useC["repQ"] = t_2
            fin_toks.append(t_2)
        st["fin_toks"] = fin_toks

    def emit_out_chunk(st, a):
        coff = st["coff"]; CW = st["CW"]
        t = E("sync", lambda e, a=a, coff=coff, CW=CW: e.dma_start(
            out=y2f[:, a, coff - HALO:coff - HALO + CW], in_=YS[:, a, 0:CW]),
            deps=st["fin_toks"], sem="yout")
        out_toks.append(t)
        if a == NC - 1:
            useC["YS"] = t

    def emit_out(st):
        for a in range(NC):
            emit_out_chunk(st, a)

    nct = len(ctiles)
    stC = {0: emit_s1C(0)}
    mprev = None
    for ci in range(nct):
        st = stC[ci]
        coff = st["coff"]; CW = st["CW"]
        d0 = b_done if ci == 0 else []
        r_toks = []
        for do in range(NC):
            t_m, ps, pn = emit_wmm(wr16, do, CW, XRK, st["xr_toks"],
                                   extra=(d0 if ci == 0 and do < 4 else []), ck="wr")
            t_s = E("scalar", lambda e, do=do, CW=CW, ps=ps: e.activation(
                RSB[:, do, 0:CW], ps[:, 0:CW], AF.Sigmoid),
                deps=[t_m] + (depC("RSB") if do == 0 else []))
            useC[pn] = t_s
            r_toks.append(t_s)
            if do == 1 and mprev is not None:
                emit_mln2_b(mprev)
            if do == 2:
                emit_xkst(st)
            if do == 4 and mprev is not None:
                emit_mln2_c(mprev)
        useC["XRK"] = r_toks[-1]
        ka_toks = []
        prev_sq = {0: None, 1: None}
        for do in range(NC):
            kt = ktmp if do % 2 == 0 else ktmp2
            t_m, ps, pn = emit_wmm(wk16, do, CW, XK2, st["xk_toks"], ck="wk")
            t_r = E("vector", lambda e, CW=CW, ps=ps, kt=kt: e.tensor_scalar_max(
                kt[:, 0:CW], ps[:, 0:CW], 0.0),
                deps=[t_m] + ([prev_sq[do % 2]] if prev_sq[do % 2] else []))
            t_q = E("scalar", lambda e, do=do, CW=CW, kt=kt: e.activation(
                KAC[:, do, 0:CW], kt[:, 0:CW], AF.Square),
                deps=[t_r] + (depC("KAC") if do == 0 else []))
            prev_sq[do % 2] = t_q
            useC[pn] = t_q
            ka_toks.append(t_q)
            if do == 1 and ci + 1 < nct:
                stC[ci + 1] = emit_s1C(ci + 1)
            if mprev is not None:
                emit_out_chunk(mprev, do)
        useC["XK2"] = ka_toks[-1]
        ys_toks = []
        prev_ty = {0: prev_sq[0], 1: prev_sq[1]}
        for do in range(NC):
            kt = ktmp if do % 2 == 0 else ktmp2
            t_m, ps, pn = emit_wmm(wv16, do, CW, KAC, ka_toks, ck="wv")
            t_u = E("vector", lambda e, do=do, CW=CW, ps=ps, kt=kt: e.tensor_tensor(
                kt[:, 0:CW], ps[:, 0:CW], RSB[:, do, 0:CW], op=ALU.mult),
                deps=[t_m] + ([prev_ty[do % 2]] if prev_ty[do % 2] else []))
            t_y = E("vector", lambda e, do=do, coff=coff, CW=CW, kt=kt: e.tensor_tensor(
                YS[:, do, 0:CW], kt[:, 0:CW], sT[:, do, coff:coff + CW], op=ALU.add),
                deps=[t_u] + (depC("YS") if do == 0 else []))
            prev_ty[do % 2] = t_y
            useC[pn] = t_y
            ys_toks.append(t_y)
        useC["RSB"] = ys_toks[-1]
        useC["KAC"] = ys_toks[-1]
        emit_mln2_a(st, ys_toks)
        mprev = st
    emit_mln2_b(mprev)
    emit_mln2_c(mprev)
    emit_out(mprev)

    esc.close()
    E("vector", lambda e: e.memset(srow_sb[0:1, 0:1], 0.0), deps=[out_toks[-1]],
      inc=True)

    with nc.Block() as block:
        sch.finalize(block)
    esg.close()
    return nc


def _perm_fm(a2d, nchunk=NC):
    Dtot, T = a2d.shape
    return np.ascontiguousarray(a2d.reshape(nchunk, 128, T).transpose(1, 0, 2))


def _pieces(wT, nin, nout):
    return np.ascontiguousarray(wT.reshape(nin, 128, nout, 128).transpose(2, 1, 0, 3))


_NC_CACHE = {}


def get_nc(key, cfg):
    if key not in _NC_CACHE:
        _NC_CACHE[key] = build_program(cfg)
    return _NC_CACHE[key]


def make_inputs(cfg, x, w):
    T_OWN = cfg["T_OWN"]; HALO = cfg["HALO"]; PPREF = cfg["PPREF"]
    n_cores = cfg["N_CORES"]; n_slice = cfg["N_SLICE"]
    f32 = np.float32
    shared = {}
    shared["identf"] = np.eye(128, dtype=f32)
    shared["ln1w8"] = w["ln1_w"].reshape(NC, 128).T
    shared["ln1b8"] = w["ln1_b"].reshape(NC, 128).T
    shared["ln2w8"] = w["ln2_w"].reshape(NC, 128).T
    shared["ln2b8"] = w["ln2_b"].reshape(NC, 128).T
    shared["mur8"] = w["mu_r"].reshape(NC, 128).T
    shared["muk8"] = w["mu_k"].reshape(NC, 128).T
    shared["ob8"] = w["o_b"].reshape(NC, 128).T
    shared["f2b8"] = w["f2_b"].reshape(NC, 128).T
    shared["f1b32"] = w["f1_b"].reshape(32, 128).T
    shared["qb_row"] = w["q_b"].reshape(1, FD)
    shared["kb_row"] = w["k_b"].reshape(1, FD)
    shared["vb_row"] = w["v_b"].reshape(1, D)
    shared["qwT"] = _perm_fm(w["q_w"].T)
    shared["kwT"] = _perm_fm(w["k_w"].T)
    shared["vwT"] = _perm_fm(w["v_w"].T)
    shared["owP"] = _pieces(w["o_w"].T, NC, NC)
    shared["f1P"] = _pieces(w["f1_w"].T, NC, 32)
    shared["f2P"] = _pieces(w["f2_w"].T, 32, NC)
    shared["wrP"] = _pieces(w["W_r"].T, NC, NC)
    shared["wkP"] = _pieces(w["W_k"].T, NC, NC)
    shared["wvP"] = _pieces(w["W_v"].T, NC, NC)
    shared = {k: np.ascontiguousarray(v, dtype=f32) for k, v in shared.items()}

    in_maps = []
    for c in range(n_cores):
        seq = c // n_slice
        s = c % n_slice
        start = s * T_OWN
        m = dict(shared)
        xm = np.zeros((T_OWN + HALO, D), dtype=f32)
        lo = start - HALO
        if lo < 0:
            xm[HALO:] = x[seq, 0:start + T_OWN]
        else:
            xm[:] = x[seq, lo:start + T_OWN]
        m["xm_f"] = _perm_fm(np.ascontiguousarray(xm.T))
        pr = np.zeros((PPREF, D), dtype=f32)
        L = max(start - HALO, 0)
        pr[:L] = x[seq, :L]
        m["xp_tok"] = np.ascontiguousarray(pr.reshape(PPREF // 128, 128, D))
        m["xp_f"] = _perm_fm(np.ascontiguousarray(pr.T))
        pm = np.zeros((1, PPREF), dtype=f32)
        pm[0, :L] = 1.0
        m["pmask"] = pm
        m["hmaskr"] = np.full((1, HALO), 1.0 if s > 0 else 0.0, dtype=f32)
        m["hscale"] = np.full((128, 1), 1.0 if s > 0 else 0.0, dtype=f32)
        in_maps.append(m)
    return in_maps


def gather_out(cfg, results, x_shape):
    T_OWN = cfg["T_OWN"]; n_slice = cfg["N_SLICE"]
    B, N, _ = x_shape
    out = np.empty((B, N, D), dtype=np.float32)
    for c in range(cfg["N_CORES"]):
        seq, s = c // n_slice, c % n_slice
        y = results[c]["y2f"]
        out[seq, s * T_OWN:(s + 1) * T_OWN] = \
            np.asarray(y).transpose(1, 0, 2).reshape(D, T_OWN).T
    return out


def kernel(L0_attn_x, L0_attn_ln1_w, L0_attn_ln1_b, L0_attn_q_w, L0_attn_q_b,
           L0_attn_k_w, L0_attn_k_b, L0_attn_v_w, L0_attn_v_b, L0_attn_o_w,
           L0_attn_o_b, L0_attn_ln2_w, L0_attn_ln2_b, L0_attn_f1_w, L0_attn_f1_b,
           L0_attn_f2_w, L0_attn_f2_b, L0_ffn_W_r, L0_ffn_W_k, L0_ffn_W_v,
           L0_ffn_mu_r, L0_ffn_mu_k):
    cfg = dict(FULL_CFG)
    x = np.asarray(L0_attn_x, dtype=np.float32)
    w = dict(ln1_w=L0_attn_ln1_w, ln1_b=L0_attn_ln1_b, q_w=L0_attn_q_w, q_b=L0_attn_q_b,
             k_w=L0_attn_k_w, k_b=L0_attn_k_b, v_w=L0_attn_v_w, v_b=L0_attn_v_b,
             o_w=L0_attn_o_w, o_b=L0_attn_o_b, ln2_w=L0_attn_ln2_w, ln2_b=L0_attn_ln2_b,
             f1_w=L0_attn_f1_w, f1_b=L0_attn_f1_b, f2_w=L0_attn_f2_w, f2_b=L0_attn_f2_b,
             W_r=L0_ffn_W_r, W_k=L0_ffn_W_k, W_v=L0_ffn_W_v, mu_r=L0_ffn_mu_r,
             mu_k=L0_ffn_mu_k)
    w = {k: np.asarray(v, dtype=np.float32) for k, v in w.items()}
    in_maps = make_inputs(cfg, x, w)
    nc = get_nc("full", cfg)
    res = run_bass_kernel_spmd(nc, in_maps, core_ids=list(range(cfg["N_CORES"])))
    return gather_out(cfg, res.results, x.shape)
